# version 19
# baseline (speedup 1.0000x reference)
import numpy as np
import concourse.bass as bass
import concourse.mybir as mybir
from concourse.bass_utils import run_bass_kernel_spmd
from contextlib import ExitStack

F32 = mybir.dt.float32
BF16 = mybir.dt.bfloat16
AF = mybir.ActivationFunctionType
ALU = mybir.AluOpType
AXX = mybir.AxisListType.X

NCORES = 8
NSEQ = 2
L = 2048
DM = 1024
DI = 2048
T = 256
NIT = L // T
W = T + 2
DMC = DM // 128
DIC = DI // 128
NW = NSEQ * T
G = 4
EPS = 1e-5
NST = 10
NSB = 4
NSTREAM = 8


def merge_toks(toks):
    best = {}
    for t in toks:
        if t is None:
            continue
        sem, val = t
        if best.get(sem, 0) < val:
            best[sem] = val
    return [(s, v) for s, v in best.items()]


class Prog:
    ENG = ("t", "v", "s", "g", "y")

    def __init__(self):
        self.stream = {e: [] for e in self.ENG}
        self.cnt = {e: 0 for e in self.ENG}
        self.dcnt = {}
        self.waited = {e: {} for e in self.ENG}

    def emit(self, eng, fn, waits=(), sig=True, dma=None):
        ws = []
        wd = self.waited[eng]
        for tok in merge_toks(waits):
            sem, val = tok
            if wd.get(sem, 0) >= val:
                continue
            wd[sem] = val
            ws.append((sem, val))
        if dma is not None:
            self.dcnt[dma] = self.dcnt.get(dma, 0) + 16
            tok = (dma, self.dcnt[dma])
        elif sig:
            self.cnt[eng] += 1
            tok = (eng, self.cnt[eng])
        else:
            tok = None
        self.stream[eng].append((ws, fn, tok))
        return tok

    def wait_only(self, eng, waits):
        self.emit(eng, None, waits=waits, sig=False)


class Buf:
    def __init__(self):
        self.w = None
        self.r = []

    def wr_waits(self):
        return merge_toks([self.w] + self.r)

    def rd_waits(self):
        return [self.w] if self.w is not None else []

    def did_write(self, tok):
        self.w = tok
        self.r = []

    def did_read(self, tok):
        if tok is not None:
            self.r.append(tok)


class Slots:
    def __init__(self, ap, n, width):
        self.ap = ap
        self.n = n
        self.width = width
        self.bufs = [Buf() for _ in range(n)]
        self.i = 0

    def next(self):
        i = self.i % self.n
        self.i += 1
        return self.ap[:, i * self.width:(i + 1) * self.width], self.bufs[i]


def build_program():
    nc = bass.Bass("TRN2", target_bir_lowering=False)
    P = Prog()

    xin = nc.dram_tensor("x", [NSEQ, DM, L], F32, kind="ExternalInput").ap()
    wu = nc.dram_tensor("wu", [DM, DI], F32, kind="ExternalInput").ap()
    wz = nc.dram_tensor("wz", [DM, DI], F32, kind="ExternalInput").ap()
    wdt = nc.dram_tensor("wdt", [DI, DI], F32, kind="ExternalInput").ap()
    wap = nc.dram_tensor("wap", [DI, DI], F32, kind="ExternalInput").ap()
    wbp = nc.dram_tensor("wbp", [DI, DI], F32, kind="ExternalInput").ap()
    wcp = nc.dram_tensor("wcp", [DI, DI], F32, kind="ExternalInput").ap()
    wo = nc.dram_tensor("wo", [DI, DM], F32, kind="ExternalInput").ap()
    wff = nc.dram_tensor("wff", [8, DM, DM], F32, kind="ExternalInput").ap()
    bff = nc.dram_tensor("bff", [8, DM], F32, kind="ExternalInput").ap()
    cw = nc.dram_tensor("cw", [128, 48], F32, kind="ExternalInput").ap()
    cb = nc.dram_tensor("cb", [128, 16], F32, kind="ExternalInput").ap()
    lg8 = nc.dram_tensor("lg8", [8], F32, kind="ExternalInput").ap()
    uu8 = nc.dram_tensor("uu8", [8], F32, kind="ExternalInput").ap()
    oout = nc.dram_tensor("o", [NSEQ, DM, L], F32, kind="ExternalOutput").ap()

    s_wu = nc.dram_tensor("s_wu", [4, 8, 128, 512], BF16, kind="Internal").ap()
    s_wz = nc.dram_tensor("s_wz", [4, 8, 128, 512], BF16, kind="Internal").ap()
    s_wdt = nc.dram_tensor("s_wdt", [4, 16, 128, 512], BF16, kind="Internal").ap()
    s_wa = nc.dram_tensor("s_wa", [4, 16, 128, 512], BF16, kind="Internal").ap()
    s_wb = nc.dram_tensor("s_wb", [4, 16, 128, 512], BF16, kind="Internal").ap()
    s_wc = nc.dram_tensor("s_wc", [4, 16, 128, 512], BF16, kind="Internal").ap()
    s_wo = nc.dram_tensor("s_wo", [2, 16, 128, 512], BF16, kind="Internal").ap()
    s_wf = nc.dram_tensor("s_wf", [2, 8, 128, 512], BF16, kind="Internal").ap()

    with ExitStack() as ctx:
        def sb(name, shape, dt):
            return ctx.enter_context(nc.sbuf_tensor(name, shape, dt))

        xw = [sb("xw0", [128, DMC * W], F32), sb("xw1", [128, DMC * W], F32)]
        xnb = [sb("xn0", [128, DMC * W], BF16), sb("xn1", [128, DMC * W], BF16)]
        u_bf = sb("u_bf", [128, DIC * NW], BF16)
        dy = sb("dy", [128, DIC * NW], BF16)
        xh = sb("xh", [128, DIC * NW], BF16)
        dsb = sb("dsb", [128, DIC * NW], BF16)
        attn = sb("attn", [128, DMC * NW], F32)
        hb = sb("hb", [128, DMC * NW], F32)
        ffb = sb("ffb", [128, DMC * NW], F32)
        sc32 = Slots(sb("sc32", [128, NST * NW], F32), NST, NW)
        bcsb = Slots(sb("bcsb", [128, 4 * NW], F32), 4, NW)
        sbf = Slots(sb("sbf", [128, NSB * 1024], BF16), NSB, 1024)
        stb = sb("stb", [128, NSTREAM * 512], BF16)
        st_bufs = [Buf() for _ in range(NSTREAM)]
        pA_in = sb("pA_in", [128, 2048], F32)
        pA_out = sb("pA_out", [128, 2048], BF16)
        psel = sb("psel", [128, 2 * 1024], BF16)
        ones_sc = sb("ones_sc", [128, 1], BF16)
        ones_row = sb("ones_row", [1, 128], BF16)
        onesT = sb("onesT", [128, NW], F32)
        cwsb = sb("cwsb", [128, 48], F32)
        cbsb = sb("cbsb", [128, 16], F32)
        xst = [sb("xst0", [128, DIC], F32), sb("xst1", [128, DIC], F32)]
        dst = [sb("dst0", [128, DIC], F32), sb("dst1", [128, DIC], F32)]
        bselS = sb("bselS", [128, 8], F32)
        iota8 = sb("iota8", [1, 8], F32)
        idxf = sb("idxf", [128, 1], F32)
        iota2d = sb("iota2d", [128, 8], F32)
        idxTf = sb("idxTf", [128, 8], F32)
        idxT = sb("idxT", [128, 8], mybir.dt.int32)
        epsC = sb("epsC", [1, 1], F32)
        meanS = sb("meanS", [1, NW], F32)
        tmpA = sb("tmpA", [1, NW], F32)
        tmpB = sb("tmpB", [1, NW], F32)
        rstdS = sb("rstdS", [1, NW], BF16)
        mrS = sb("mrS", [1, NW], BF16)
        t8a = sb("t8a", [1, 8], F32)
        t8b = sb("t8b", [1, 8], F32)
        t8c = sb("t8c", [1, 8], F32)
        t8m = sb("t8m", [1, 1], F32)
        t8n = sb("t8n", [1, 1], F32)
        t8oh = sb("t8oh", [1, 8], F32)
        t8ohb = sb("t8ohb", [1, 8], BF16)

        banks = []
        for i in range(8):
            ap = ctx.enter_context(nc.psum_tensor(f"bank{i}", [128, 512], F32))
            banks.append({"ap": ap, "buf": Buf(), "held": False})
        bank_ptr = [0]

        def take_banks(n):
            out = []
            spins = 0
            while len(out) < n:
                b = banks[bank_ptr[0] % 8]
                bank_ptr[0] += 1
                spins += 1
                assert spins < 64, "bank pool exhausted (too many held banks)"
                if b["held"] or b in out:
                    continue
                out.append(b)
            return out

        DMA_SEMS = (["sty%d" % i for i in range(NSTREAM)] +
                    ["stg%d" % i for i in range(NSTREAM)] +
                    ["dx0", "dx1", "do", "wai", "wao0", "wao1",
                     "xp0", "xp1", "wfo", "wfo1", "dc", "dr"] +
                    ["c_wu%d" % i for i in range(4)] +
                    ["c_wdt%d" % i for i in range(4)] +
                    ["c_wb%d" % i for i in range(4)] +
                    ["c_wc%d" % i for i in range(4)] +
                    ["c_wz%d" % i for i in range(4)] +
                    ["c_wo%d" % i for i in range(2)])
        sems = {}
        for nm in ["t", "v", "s", "g"] + DMA_SEMS:
            sems[nm] = ctx.enter_context(nc.semaphore(nm))
        DMA_SEMS = set(DMA_SEMS)
        block = ctx.enter_context(nc.Block())

        def dma(queue, out_ap, in_ap, waits=(), sem=None):
            return P.emit(queue, (lambda e, o=out_ap, i=in_ap: e.dma_start(o, i)),
                          waits=waits, dma=sem)

        cur_it = [0]
        stream_q = [0]

        def load_st(src_ap, waits=()):
            i = stream_q[0] % NSTREAM
            stream_q[0] += 1
            slot = st_bufs[i]
            ap = stb[:, i * 512:(i + 1) * 512]
            tok = P.emit("y", (lambda e, o=ap, s2=src_ap: e.dma_start(o, s2)),
                         waits=list(waits) + slot.wr_waits(),
                         dma="sty%d" % i)
            slot.did_write(tok)
            return ap, tok, slot

        def mm_group(scratch_blk, K, mslots, rhs_fn, rhs_tok_fn, cast_tok,
                     out_lo=0, out_hi=NW):
            ng = len(mslots)
            grp = take_banks(ng)
            gtok = None
            cast_w = (list(cast_tok) if isinstance(cast_tok, list)
                      else [cast_tok])
            for k in range(K):
                ap, tok, slot = load_st(scratch_blk(k), waits=cast_w)
                waits = [tok] + (rhs_tok_fn(k) if rhs_tok_fn else [])
                rhs = rhs_fn(k)
                for j in range(ng):
                    b = grp[j]
                    last = (j == ng - 1)
                    w = list(waits) if j == 0 else []
                    if k == 0:
                        w += b["buf"].wr_waits()
                    kt = P.emit("t", (lambda e, o=b["ap"][:, out_lo:out_hi],
                                      lh=ap[:, j * 128:(j + 1) * 128], r=rhs,
                                      st=(k == 0), sp=(k == K - 1):
                                      e.matmul(o, lh, r, start=st, stop=sp)),
                                waits=w, sig=last)
                    if last:
                        slot.did_read(kt)
                        gtok = kt
            for b in grp:
                b["buf"].did_write(gtok)
            return grp, gtok

        chainB = {nm: Buf() for nm in ("mean", "tmpA", "tmpB", "rstd", "mr")}

        class LNAcc:
            def __init__(self, n, width):
                self.n = n
                self.width = width
                self.c = 0
                self.bsum, self.bsq = take_banks(2)
                self.bsum["held"] = True
                self.bsq["held"] = True
                self.last = [None, None]

            def add(self, slot, t_data, t_sq, sbuf_):
                c, n, width = self.c, self.n, self.width
                m1 = P.emit("t", (lambda e, o=self.bsum["ap"][0:1, 0:width],
                                  r=slot[:, 0:width], st=(c == 0),
                                  sp=(c == n - 1):
                                  e.matmul(o, ones_sc[:, 0:1], r,
                                           start=st, stop=sp)),
                            waits=[t_data, k0] +
                            (self.bsum["buf"].wr_waits() if c == 0 else []))
                m2 = P.emit("t", (lambda e, o=self.bsq["ap"][0:1, 0:width],
                                  r=slot[:, 512:512 + width], st=(c == 0),
                                  sp=(c == n - 1):
                                  e.matmul(o, ones_sc[:, 0:1], r,
                                           start=st, stop=sp)),
                            waits=[t_sq] +
                            (self.bsq["buf"].wr_waits() if c == 0 else []))
                sbuf_.did_read(m1)
                sbuf_.did_read(m2)
                self.last = [m1, m2]
                self.c += 1

            def finish(self):
                width = self.width
                bsum, bsq = self.bsum, self.bsq
                bsum["held"] = False
                bsq["held"] = False
                bsum["buf"].did_write(self.last[0])
                bsq["buf"].did_write(self.last[1])
                c1 = P.emit("v", (lambda e: e.tensor_copy(
                    meanS[0:1, 0:width], bsum["ap"][0:1, 0:width])),
                    waits=[self.last[0]] + chainB["mean"].wr_waits())
                bsum["buf"].did_read(c1)
                chainB["mean"].did_write(c1)
                c2 = P.emit("s", (lambda e: e.activation(
                    tmpA[0:1, 0:width], meanS[0:1, 0:width], AF.Square)),
                    waits=[c1] + chainB["tmpA"].wr_waits())
                chainB["mean"].did_read(c2)
                chainB["tmpA"].did_write(c2)
                c3 = P.emit("v", (lambda e: e.tensor_tensor(
                    tmpB[0:1, 0:width], bsq["ap"][0:1, 0:width],
                    tmpA[0:1, 0:width], op=ALU.subtract)),
                    waits=[self.last[1], c2] + chainB["tmpB"].wr_waits())
                bsq["buf"].did_read(c3)
                chainB["tmpA"].did_read(c3)
                chainB["tmpB"].did_write(c3)
                c4 = P.emit("s", (lambda e: e.activation(
                    tmpA[0:1, 0:width], tmpB[0:1, 0:width], AF.Ln,
                    bias=epsC[0:1, 0:1])),
                    waits=[c3, kE] + chainB["tmpA"].wr_waits())
                chainB["tmpB"].did_read(c4)
                chainB["tmpA"].did_write(c4)
                c5 = P.emit("s", (lambda e: e.activation(
                    rstdS[0:1, 0:width], tmpA[0:1, 0:width], AF.Exp,
                    scale=-0.5)),
                    waits=[c4] + chainB["rstd"].wr_waits())
                chainB["tmpA"].did_read(c5)
                chainB["rstd"].did_write(c5)
                c6 = P.emit("v", (lambda e: e.tensor_tensor(
                    mrS[0:1, 0:width], meanS[0:1, 0:width],
                    rstdS[0:1, 0:width], op=ALU.mult)),
                    waits=[c5] + chainB["mr"].wr_waits())
                chainB["mean"].did_read(c6)
                chainB["rstd"].did_read(c6)
                chainB["mr"].did_write(c6)
                brs, brm = take_banks(2)
                b1 = P.emit("t", (lambda e: e.matmul(
                    brs["ap"][:, 0:width], ones_row[0:1, 0:128],
                    rstdS[0:1, 0:width], start=True, stop=True)),
                    waits=[c5, k1] + brs["buf"].wr_waits())
                chainB["rstd"].did_read(b1)
                b2 = P.emit("t", (lambda e: e.matmul(
                    brm["ap"][:, 0:width], ones_row[0:1, 0:128],
                    mrS[0:1, 0:width], start=True, stop=True)),
                    waits=[c6] + brm["buf"].wr_waits())
                chainB["mr"].did_read(b2)
                brs["buf"].did_write(b1)
                brm["buf"].did_write(b2)
                slr, sbr = bcsb.next()
                e1 = P.emit("s", (lambda e: e.activation(
                    slr[:, 0:width], brs["ap"][:, 0:width], AF.Identity)),
                    waits=[b1] + sbr.wr_waits())
                brs["buf"].did_read(e1)
                sbr.did_write(e1)
                slm, sbm = bcsb.next()
                e2 = P.emit("s", (lambda e: e.activation(
                    slm[:, 0:width], brm["ap"][:, 0:width], AF.Identity)),
                    waits=[b2] + sbm.wr_waits())
                brm["buf"].did_read(e2)
                sbm.did_write(e2)
                return ({"ap": slr, "buf": sbr}, {"ap": slm, "buf": sbm},
                        e1, e2)

        k0 = P.emit("v", lambda e: e.memset(ones_sc[:, :], 1.0 / DM))
        k1 = P.emit("v", lambda e: e.memset(ones_row[:, :], 1.0))
        k2 = P.emit("v", lambda e: e.memset(onesT[:, :], 1.0))
        kE = P.emit("v", lambda e: e.memset(epsC[:, :], float(EPS)))
        dma("g", cwsb[:, :], cw, sem="dc")
        dma("g", cbsb[:, :], cb, sem="dc")
        dma("g", t8a[:, :], lg8.rearrange("(a b) -> a b", a=1), sem="dc")
        tok_dc = dma("g", t8b[:, :], uu8.rearrange("(a b) -> a b", a=1),
                     sem="dc")
        tcw = tcb = tlg = tuu = tok_dc

        cast = {}

        def cast_stage(name, src, dstap, n_mg):
            n_k = dstap.shape[1]
            view = src.rearrange("(k p) (mg q) -> k mg p q", p=128, q=512)
            toks = [dma("g", dstap[:, k], view[k],
                        sem="c_%s%d" % (name, k % n_mg))
                    for k in range(n_k)]
            cast[name] = [toks] * n_mg

        g1 = P.emit("s", lambda e: e.activation(t8c[:, :], t8b[:, :], AF.Ln),
                    waits=[tuu])
        g2 = P.emit("s", lambda e: e.activation(t8b[:, :], t8c[:, :], AF.Ln,
                                                scale=-1.0), waits=[g1])
        g3 = P.emit("v", lambda e: e.tensor_reduce(t8m[:, :], t8a[:, :],
                                                   axis=AXX, op=ALU.max),
                    waits=[tlg])
        g4 = P.emit("v", lambda e: e.tensor_single_scalar(t8c[:, :], t8a[:, :],
                                                          t8m[0:1, 0:1],
                                                          op=ALU.subtract),
                    waits=[g3, g2])
        g5 = P.emit("s", lambda e: e.activation(t8oh[:, :], t8c[:, :], AF.Exp),
                    waits=[g4])
        g6 = P.emit("v", lambda e: e.tensor_reduce(t8n[:, :], t8oh[:, :],
                                                   axis=AXX, op=ALU.add),
                    waits=[g5])
        g7 = P.emit("s", lambda e: e.activation(t8n[:, :], t8n[:, :], AF.Ln),
                    waits=[g6])
        g8 = P.emit("v", lambda e: e.tensor_tensor(t8c[:, :], t8a[:, :],
                                                   t8b[:, :], op=ALU.subtract),
                    waits=[g2, g5, tlg])
        g9 = P.emit("v", lambda e: e.tensor_single_scalar(t8c[:, :], t8c[:, :],
                                                          t8n[0:1, 0:1],
                                                          op=ALU.subtract),
                    waits=[g7, g8])
        g11 = P.emit("v", lambda e: e.tensor_single_scalar(t8c[:, :],
                                                           t8c[:, :],
                                                           t8m[0:1, 0:1],
                                                           op=ALU.subtract),
                     waits=[g9])
        g12 = P.emit("v", lambda e: e.tensor_reduce(t8m[:, :], t8c[:, :],
                                                    axis=AXX, op=ALU.max),
                     waits=[g11])
        g13 = P.emit("v", lambda e: e.tensor_single_scalar(t8oh[:, :],
                                                           t8c[:, :],
                                                           t8m[0:1, 0:1],
                                                           op=ALU.is_ge),
                     waits=[g12, g6])
        gi8 = P.emit("g", lambda e: e.iota(iota8[:, :], [[1, 8]],
                                           channel_multiplier=0,
                                           allow_small_or_imprecise_dtypes=True))
        gi2 = P.emit("g", lambda e: e.iota(iota2d[:, :], [[128, 8]],
                                           channel_multiplier=1,
                                           allow_small_or_imprecise_dtypes=True))
        g14 = P.emit("v", lambda e: e.tensor_tensor(t8c[:, :], t8oh[:, :],
                                                    iota8[:, :], op=ALU.mult),
                     waits=[g13, gi8])
        g15 = P.emit("v", lambda e: e.tensor_reduce(t8m[:, :], t8c[:, :],
                                                    axis=AXX, op=ALU.add),
                     waits=[g14])
        g16 = P.emit("v", lambda e: e.tensor_single_scalar(t8ohb[0:1, 0:1],
                                                           t8m[:, :], 1024.0,
                                                           op=ALU.mult),
                     waits=[g15])
        boh = take_banks(1)[0]
        g17 = P.emit("t", lambda e: e.matmul(boh["ap"][:, 0:1],
                                             ones_row[0:1, 0:128],
                                             t8ohb[0:1, 0:1],
                                             start=True, stop=True),
                     waits=[g16, k1] + boh["buf"].wr_waits())
        boh["buf"].did_write(g17)
        g18 = P.emit("v", lambda e: e.tensor_copy(idxf[:, :],
                                                  boh["ap"][:, 0:1]),
                     waits=[g17])
        boh["buf"].did_read(g18)
        g18b = P.emit("v", lambda e: e.tensor_scalar(idxTf[:, :], iota2d[:, :],
                                                     idxf[:, 0:1], None,
                                                     op0=ALU.add),
                      waits=[g18, gi2])
        g19i = P.emit("v", lambda e: e.tensor_copy(idxT[:, :], idxTf[:, :]),
                      waits=[g18b])
        bffv = bff.rearrange("a (b c) -> (a b) c", c=1)
        bsel_toks = []
        for m in range(8):
            tm = P.emit("g", (lambda e, m=m: e.indirect_dma_start(
                bselS[:, m:m + 1], None, bffv,
                bass.IndirectOffsetOnAxis(idxT[:, m:m + 1], 0))),
                waits=[g19i], dma="xp0")
            bsel_toks.append(tm)
        g19 = bsel_toks[-1]
        wffv = wff.rearrange("e d c -> (e d) c")
        pselB = [Buf(), Buf()]
        ffslotB = [Buf(), Buf()]
        wf_last = None
        for kd in range(8):
            sl = kd % 2
            tile = ffb[:, sl * 1024:(sl + 1) * 1024]
            tin = P.emit("g", (lambda e, t2=tile, kd=kd: e.indirect_dma_start(
                t2, None, wffv,
                bass.IndirectOffsetOnAxis(idxT[:, kd:kd + 1], 0))),
                waits=[g19i] + ffslotB[sl].wr_waits(),
                dma=("xp1" if sl == 0 else "dr"))
            ffslotB[sl].did_write(tin)
            outt = psel[:, sl * 1024:(sl + 1) * 1024]
            tc = P.emit("s", (lambda e, o=outt, a=tile:
                              e.activation(o, a, AF.Identity)),
                        waits=[tin] + pselB[sl].wr_waits())
            ffslotB[sl].did_read(tc)
            pselB[sl].did_write(tc)
            dstv = s_wf[0:2, kd].rearrange("mg p q -> p mg q")
            tw = P.emit("y", (lambda e, o=dstv,
                              a=outt.rearrange("p (mg q) -> p mg q", q=512):
                              e.dma_start(o, a)),
                        waits=[tc], dma=("wfo" if sl == 0 else "wfo1"))
            pselB[sl].did_read(tw)
        wf_done = [("wfo", P.dcnt["wfo"]), ("wfo1", P.dcnt["wfo1"])]
        cast["wf"] = [wf_done, wf_done]
        ff_prolog_reads = list(wf_done)

        cast_stage("wu", wu, s_wu, 4)

        expa_state = {"in": Buf(), "out": Buf(), "k": 0, "toks": []}

        def emit_expa_piece():
            k = expa_state["k"]
            if k >= 16:
                return
            expa_state["k"] += 1
            tin = P.emit("s", (lambda e, o=pA_in[:, :],
                               s2=wap[k * 128:(k + 1) * 128, :]:
                               e.dma_start(o, s2)),
                         waits=expa_state["in"].wr_waits(), dma="wai")
            expa_state["in"].did_write(tin)
            te = P.emit("s", (lambda e, o=pA_out[:, :], a=pA_in[:, :]:
                              e.activation(o, a, AF.Exp)),
                        waits=[tin] + expa_state["out"].wr_waits())
            expa_state["in"].did_read(te)
            expa_state["out"].did_write(te)
            tw = P.emit("s", (lambda e,
                              o=s_wa[:, k].rearrange("mg p q -> p mg q"),
                              a=pA_out[:, :].rearrange("p (mg q) -> p mg q",
                                                       q=512):
                              e.dma_start(o, a)),
                        waits=[te], dma="wao0")
            expa_state["out"].did_read(tw)
            expa_state["toks"].append(tw)
        cast["wa"] = None

        xw_bufs = [Buf(), Buf()]
        xn_bufs = [Buf(), Buf()]
        ubf_bufs = [Buf() for _ in range(DIC)]
        dy_bufs = [Buf() for _ in range(DIC)]
        xh_bufs = [Buf() for _ in range(DIC)]
        ds_bufs = [Buf() for _ in range(DIC)]
        attn_bufs = [Buf() for _ in range(DMC)]
        hb_bufs = [Buf() for _ in range(DMC)]
        ff_bufs = [Buf() for _ in range(DMC)]
        st_state = [[Buf() for _ in range(DIC)], [Buf() for _ in range(DIC)]]
        ds_state = [[Buf() for _ in range(DIC)], [Buf() for _ in range(DIC)]]
        out_dma_toks = []

        xview = [xin[s].rearrange("(c p) t -> p c t", p=128)
                 for s in range(NSEQ)]
        oview = [oout[s].rearrange("(c p) t -> p c t", p=128)
                 for s in range(NSEQ)]

        def emit_head(it):
            tk = it * T
            xw_wtoks = [[], []]
            for s in range(NSEQ):
                xwv = xw[s][:, :].rearrange("p (c w) -> p c w", w=W)
                war = xw_bufs[s].wr_waits()
                if it == 0:
                    t1 = dma("y", xwv[:, :, 1:W], xview[s][:, :, 0:W - 1],
                             waits=war, sem="dx%d" % s)
                    t2 = P.emit("v", (lambda e, o=xwv[:, :, 0:1]:
                                      e.memset(o, 0.0)), waits=war)
                    xw_wtoks[s] = [t1, t2]
                elif it == NIT - 1:
                    t1 = dma("y", xwv[:, :, 0:W - 1],
                             xview[s][:, :, tk - 1:L], waits=war,
                             sem="dx%d" % s)
                    t2 = P.emit("v", (lambda e, o=xwv[:, :, W - 1:W]:
                                      e.memset(o, 0.0)), waits=war)
                    xw_wtoks[s] = [t1, t2]
                else:
                    t1 = dma("y", xwv[:, :, :],
                             xview[s][:, :, tk - 1:tk + W - 1], waits=war,
                             sem="dx%d" % s)
                    xw_wtoks[s] = [t1]
                xw_bufs[s].w = xw_wtoks[s][0]
                xw_bufs[s].r = list(xw_wtoks[s][1:])
            heads = []
            for s in range(NSEQ):
                acc1 = LNAcc(DMC, W)
                for c in range(DMC):
                    slot, sbuf_ = sbf.next()
                    tq = P.emit("g", (lambda e, o=slot[:, 0:W],
                                      a=xw[s][:, c * W:(c + 1) * W]:
                                      e.tensor_copy(o, a)),
                                waits=xw_wtoks[s] + sbuf_.wr_waits())
                    tsq = P.emit("v", (lambda e, o=slot[:, 512:512 + W],
                                       a=xw[s][:, c * W:(c + 1) * W]:
                                       e.tensor_tensor(o, a, a, op=ALU.mult)),
                                 waits=xw_wtoks[s] + sbuf_.wr_waits())
                    xw_bufs[s].did_read(tq)
                    xw_bufs[s].did_read(tsq)
                    sbuf_.w = tsq
                    sbuf_.r = [tq]
                    acc1.add(slot, tq, tsq, sbuf_)
                brs, brm, b1, b2 = acc1.finish()
                heads.append((brs, brm, b1, b2))
            return {"xw_wtoks": xw_wtoks, "heads": heads}

        def emit_ln1_apply(head):
            xw_wtoks = head["xw_wtoks"]
            xn_tok = [None, None]
            xn_ctoks = [[None] * DMC, [None] * DMC]
            for s in range(NSEQ):
                brs, brm, b1, b2 = head["heads"][s]
                atoks = []
                for c in range(DMC):
                    slot, sbuf_ = sc32.next()
                    a1 = P.emit("v", (lambda e, o=slot[:, 0:W],
                                      a=xw[s][:, c * W:(c + 1) * W],
                                      b3=brs["ap"][:, 0:W]:
                                      e.tensor_tensor(o, a, b3, op=ALU.mult)),
                                waits=xw_wtoks[s] + [b1] + sbuf_.wr_waits())
                    brs["buf"].did_read(a1)
                    xw_bufs[s].did_read(a1)
                    sbuf_.did_write(a1)
                    a2 = P.emit("v", (lambda e,
                                      o=xnb[s][:, c * W:(c + 1) * W],
                                      a=slot[:, 0:W], b3=brm["ap"][:, 0:W]:
                                      e.tensor_tensor(o, a, b3,
                                                      op=ALU.subtract)),
                                waits=[a1, b2] +
                                (xn_bufs[s].wr_waits() if c == 0 else []))
                    brm["buf"].did_read(a2)
                    sbuf_.did_read(a2)
                    atoks.append(a2)
                    xn_ctoks[s][c] = a2
                xn_bufs[s].did_write(atoks[-1])
                xn_tok[s] = atoks[-1]
            return xn_tok, xn_ctoks

        pending_tail = [None]

        def flush_tail():
            if pending_tail[0] is None:
                return
            acc4_, tk_ = pending_tail[0]
            pending_tail[0] = None
            brs4, brm4, b1_4, b2_4 = acc4_.finish()
            out_toks = []
            for m in range(DMC):
                slot, sbuf_ = sc32.next()
                a1 = P.emit("v", (lambda e, o=slot[:, 0:NW],
                                  a=ffb[:, m * NW:(m + 1) * NW],
                                  b3=brs4["ap"][:, 0:NW]:
                                  e.tensor_tensor(o, a, b3, op=ALU.mult)),
                            waits=[ff_bufs[m].w, b1_4] + sbuf_.wr_waits())
                brs4["buf"].did_read(a1)
                ff_bufs[m].did_read(a1)
                sbuf_.did_write(a1)
                a2 = P.emit("v", (lambda e, o=slot[:, 0:NW], a=slot[:, 0:NW],
                                  b3=brm4["ap"][:, 0:NW]:
                                  e.tensor_tensor(o, a, b3, op=ALU.subtract)),
                            waits=[a1, b2_4])
                brm4["buf"].did_read(a2)
                a3 = P.emit("v", (lambda e, o=attn[:, m * NW:(m + 1) * NW],
                                  a=slot[:, 0:NW],
                                  b3=hb[:, m * NW:(m + 1) * NW]:
                                  e.tensor_tensor(o, a, b3, op=ALU.add)),
                            waits=[a2, hb_bufs[m].w] +
                            attn_bufs[m].wr_waits())
                hb_bufs[m].did_read(a3)
                sbuf_.did_read(a3)
                attn_bufs[m].did_write(a3)
                out_toks.append(a3)
            av = attn[:, :].rearrange("p (c w) -> p c w", w=NW)
            for s in range(NSEQ):
                tok = dma("s", oview[s][:, :, tk_:tk_ + T],
                          av[:, :, s * T:(s + 1) * T], waits=out_toks,
                          sem="do")
                out_dma_toks.append(tok)
                for m in range(DMC):
                    attn_bufs[m].did_read(tok)

        head = emit_head(0)
        xn_tok, xn_ctoks = emit_ln1_apply(head)
        cast_stage("wdt", wdt, s_wdt, 4)

        for it in range(NIT):
            cur_it[0] = it
            tk = it * T
            xw_wtoks = head["xw_wtoks"]

            for s in range(NSEQ):
                for mg in range(4):
                    grp, gtok = mm_group(
                        (lambda k, mg=mg: s_wu[mg, k]),
                        8, list(range(mg * G, (mg + 1) * G)),
                        rhs_fn=lambda k: xnb[s][:, k * W:(k + 1) * W],
                        rhs_tok_fn=lambda k: [xn_tok[s]],
                        cast_tok=cast["wu"][mg], out_lo=0, out_hi=W)
                    xn_bufs[s].did_read(gtok)
                    for j in range(G):
                        m = mg * G + j
                        b = grp[j]
                        slot, sbuf_ = sc32.next()
                        acc = slot[:, 0:T]
                        v1 = P.emit("v", (lambda e, o=acc,
                                          a=b["ap"][:, 0:T],
                                          w2=cwsb[:, 3 * m:3 * m + 1]:
                                          e.tensor_scalar(o, a, w2, None,
                                                          op0=ALU.mult)),
                                    waits=[gtok, tcw] + sbuf_.wr_waits())
                        v2 = P.emit("v", (lambda e, o=acc,
                                          a=b["ap"][:, 1:T + 1],
                                          w2=cwsb[:, 3 * m + 1:3 * m + 2]:
                                          e.scalar_tensor_tensor(
                                              o, a, w2, o, op0=ALU.mult,
                                              op1=ALU.add)), waits=[v1])
                        v3 = P.emit("v", (lambda e, o=acc,
                                          a=b["ap"][:, 2:T + 2],
                                          w2=cwsb[:, 3 * m + 2:3 * m + 3]:
                                          e.scalar_tensor_tensor(
                                              o, a, w2, o, op0=ALU.mult,
                                              op1=ALU.add)), waits=[v2])
                        b["buf"].did_read(v3)
                        sbuf_.did_write(v3)
                        ucol = m * NW + s * T
                        s1 = P.emit("s", (lambda e, o=u_bf[:, ucol:ucol + T],
                                          a=acc, bi=cbsb[:, m:m + 1]:
                                          e.activation(o, a, AF.Silu,
                                                       bias=bi)),
                                    waits=[v3, tcb] +
                                    (ubf_bufs[m].wr_waits() if s == 0
                                     else []))
                        sbuf_.did_read(s1)
                        if s == 0:
                            ubf_bufs[m].did_write(s1)
                        else:
                            ubf_bufs[m].r.append(s1)
                    emit_expa_piece()
                    emit_expa_piece()
            if cast["wa"] is None:
                cast["wa"] = [[("wao0", P.dcnt["wao0"])]] * 4
                cast_stage("wb", wbp, s_wb, 4)
                cast_stage("wc", wcp, s_wc, 4)
            flush_tail()
            u_ready = [merge_toks([ubf_bufs[m].w] + ubf_bufs[m].r)
                       for m in range(DIC)]

            for mg in range(4):
                grp, gtok = mm_group(
                    (lambda k, mg=mg: s_wdt[mg, k]),
                    16, list(range(mg * G, (mg + 1) * G)),
                    rhs_fn=lambda k: u_bf[:, k * NW:(k + 1) * NW],
                    rhs_tok_fn=lambda k: u_ready[k],
                    cast_tok=cast["wdt"][mg])
                for kk in range(16):
                    ubf_bufs[kk].did_read(gtok)
                for j in range(G):
                    m = mg * G + j
                    b = grp[j]
                    slot, sbuf_ = sc32.next()
                    e1 = P.emit("s", (lambda e, o=slot[:, 0:NW],
                                      a=b["ap"][:, 0:NW]:
                                      e.activation(o, a, AF.Exp)),
                                waits=[gtok] + sbuf_.wr_waits())
                    b["buf"].did_read(e1)
                    sbuf_.did_write(e1)
                    e2 = P.emit("s", (lambda e, o=dy[:, m * NW:(m + 1) * NW],
                                      a=slot[:, 0:NW]:
                                      e.activation(o, a, AF.Ln, bias=1.0)),
                                waits=[e1] + dy_bufs[m].wr_waits())
                    sbuf_.did_read(e2)
                    dy_bufs[m].did_write(e2)
            if it == 0:
                cast_stage("wz", wz, s_wz, 4)
                cast_stage("wo", wo, s_wo, 2)
            delta_ready = [dy_bufs[m].rd_waits() for m in range(DIC)]

            for mg in range(4):
                ms = list(range(mg * G, (mg + 1) * G))
                grp_a, gtok_a = mm_group(
                    (lambda k, mg=mg: s_wa[mg, k]),
                    16, ms,
                    rhs_fn=lambda k: dy[:, k * NW:(k + 1) * NW],
                    rhs_tok_fn=lambda k: delta_ready[k],
                    cast_tok=cast["wa"][mg])
                grp_b, gtok_b = mm_group(
                    (lambda k, mg=mg: s_wb[mg, k]),
                    16, ms,
                    rhs_fn=lambda k: dy[:, k * NW:(k + 1) * NW],
                    rhs_tok_fn=lambda k: delta_ready[k],
                    cast_tok=cast["wb"][mg])
                for kk in range(16):
                    dy_bufs[kk].did_read(gtok_a)
                    dy_bufs[kk].did_read(gtok_b)
                xs_info = []
                for j, m in enumerate(ms):
                    ba, bb = grp_a[j], grp_b[j]
                    slA, bufA = sc32.next()
                    dA = slA[:, 0:NW]
                    e1 = P.emit("s", (lambda e, o=dA, a=ba["ap"][:, 0:NW]:
                                      e.activation(o, a, AF.Exp, scale=-1.0)),
                                waits=[gtok_a] + bufA.wr_waits())
                    ba["buf"].did_read(e1)
                    bufA.did_write(e1)
                    slB, bufB = sc32.next()
                    dBu = slB[:, 0:NW]
                    ucol = m * NW
                    e2 = P.emit("v", (lambda e, o=dBu, a=bb["ap"][:, 0:NW],
                                      b3=u_bf[:, ucol:ucol + NW]:
                                      e.tensor_tensor(o, a, b3, op=ALU.mult)),
                                waits=[gtok_b] + u_ready[m] + bufB.wr_waits())
                    bb["buf"].did_read(e2)
                    ubf_bufs[m].did_read(e2)
                    bufB.did_write(e2)
                    slX, bufX = sc32.next()
                    xs_ = slX[:, 0:NW]
                    sc_toks = []
                    for s in range(NSEQ):
                        lo, hi = s * T, (s + 1) * T
                        if it == 0:
                            init, iw = 0.0, []
                        else:
                            init = xst[s][:, m:m + 1]
                            iw = st_state[s][m].rd_waits()
                        sct = P.emit("v", (lambda e, o=xs_[:, lo:hi],
                                           a=dA[:, lo:hi], b3=dBu[:, lo:hi],
                                           ii=init:
                                           e.tensor_tensor_scan(
                                               o, a, b3, ii, op0=ALU.mult,
                                               op1=ALU.add)),
                                     waits=[e1, e2] + iw +
                                     (bufX.wr_waits() if s == 0 else []))
                        sc_toks.append(sct)
                        stt = P.emit("v", (lambda e, o=xst[s][:, m:m + 1],
                                           a=xs_[:, hi - 1:hi]:
                                           e.tensor_copy(o, a)),
                                     waits=[sct] +
                                     st_state[s][m].wr_waits())
                        st_state[s][m].did_write(stt)
                    bufA.did_read(sc_toks[-1])
                    bufB.did_read(sc_toks[-1])
                    bufX.w = sc_toks[-1]
                    bufX.r = [sc_toks[0]]
                    p1 = P.emit("g", (lambda e,
                                      o=xh[:, m * NW:(m + 1) * NW], a=xs_:
                                      e.tensor_copy(o, a)),
                                waits=sc_toks + xh_bufs[m].wr_waits())
                    bufX.did_read(p1)
                    xh_bufs[m].did_write(p1)
                    xs_info.append((m, xs_, bufX, sc_toks))
                for m, xs_, bufX, sc_toks in xs_info:
                    ucol = m * NW
                    slT, bufT = sc32.next()
                    t2_ = slT[:, 0:NW]
                    p2 = P.emit("v", (lambda e, o=t2_,
                                      a=u_bf[:, ucol:ucol + NW], b3=xs_:
                                      e.tensor_tensor(o, a, b3, op=ALU.mult)),
                                waits=sc_toks + u_ready[m] + bufT.wr_waits())
                    bufX.did_read(p2)
                    ubf_bufs[m].did_read(p2)
                    bufT.did_write(p2)
                    dtoks = []
                    for s in range(NSEQ):
                        lo, hi = s * T, (s + 1) * T
                        if it == 0:
                            init, iw = 1.0, []
                        else:
                            init = dst[s][:, m:m + 1]
                            iw = ds_state[s][m].rd_waits()
                        dct = P.emit("v", (lambda e,
                                           o=dsb[:, m * NW + lo:m * NW + hi],
                                           a=onesT[:, lo:hi],
                                           b3=t2_[:, lo:hi], ii=init:
                                           e.tensor_tensor_scan(
                                               o, a, b3, ii, op0=ALU.mult,
                                               op1=ALU.add)),
                                     waits=[p2, k2] + iw +
                                     (ds_bufs[m].wr_waits() if s == 0
                                      else []))
                        dtoks.append(dct)
                        stt = P.emit("v", (lambda e, o=dst[s][:, m:m + 1],
                                           a=dsb[:, m * NW + hi - 1:
                                                 m * NW + hi]:
                                           e.tensor_copy(o, a)),
                                     waits=[dct] +
                                     ds_state[s][m].wr_waits())
                        ds_state[s][m].did_write(stt)
                    bufT.did_read(dtoks[-1])
                    ds_bufs[m].w = dtoks[-1]
                    ds_bufs[m].r = [dtoks[0]]
            xs_ready = [xh_bufs[m].rd_waits() for m in range(DIC)]

            for mg in range(4):
                ms = list(range(mg * G, (mg + 1) * G))
                grp_c, gtok_c = mm_group(
                    (lambda k, mg=mg: s_wc[mg, k]),
                    16, ms,
                    rhs_fn=lambda k: xh[:, k * NW:(k + 1) * NW],
                    rhs_tok_fn=lambda k: xs_ready[k],
                    cast_tok=cast["wc"][mg])
                for kk in range(16):
                    xh_bufs[kk].did_read(gtok_c)
                grp_z = take_banks(G)
                sts = [load_st(s_wz[mg, k], waits=list(cast["wz"][mg]))
                       for k in range(8)]
                gtok_z = None
                for k in range(8):
                    ap, tok, slot = sts[k]
                    waits = [tok, xn_tok[0], xn_tok[1]]
                    for j in range(G):
                        b = grp_z[j]
                        for s in range(NSEQ):
                            rhs = xnb[s][:, k * W + 1:k * W + 1 + T]
                            last = (j == G - 1 and s == NSEQ - 1)
                            w = (list(waits) if (j == 0 and s == 0)
                                 else [])
                            if k == 0 and s == 0:
                                w += b["buf"].wr_waits()
                            kt = P.emit("t", (lambda e,
                                              o=b["ap"][:, s * T:(s + 1) * T],
                                              lh=ap[:, j * 128:(j + 1) * 128],
                                              r=rhs,
                                              st=(k == 0 and s == 0),
                                              sp=(k == 7 and s == NSEQ - 1):
                                              e.matmul(o, lh, r, start=st,
                                                       stop=sp)),
                                        waits=w, sig=last)
                            if last:
                                slot.did_read(kt)
                                gtok_z = kt
                for b in grp_z:
                    b["buf"].did_write(gtok_z)
                xn_bufs[0].did_read(gtok_z)
                xn_bufs[1].did_read(gtok_z)
                for j, m in enumerate(ms):
                    bc_, bz_ = grp_c[j], grp_z[j]
                    slT, bufT = sc32.next()
                    tz = slT[:, 0:NW]
                    y1 = P.emit("v", (lambda e, o=tz, a=bz_["ap"][:, 0:NW],
                                      b3=dsb[:, m * NW:(m + 1) * NW]:
                                      e.tensor_tensor(o, a, b3, op=ALU.mult)),
                                waits=[gtok_z, ds_bufs[m].w] + ds_bufs[m].r +
                                bufT.wr_waits())
                    bz_["buf"].did_read(y1)
                    ds_bufs[m].did_read(y1)
                    bufT.did_write(y1)
                    y2 = P.emit("v", (lambda e,
                                      o=dy[:, m * NW:(m + 1) * NW], a=tz,
                                      b3=bc_["ap"][:, 0:NW]:
                                      e.tensor_tensor(o, a, b3, op=ALU.mult)),
                                waits=[y1, gtok_c] + dy_bufs[m].wr_waits())
                    bc_["buf"].did_read(y2)
                    bufT.did_read(y2)
                    dy_bufs[m].did_write(y2)
            y_ready = [dy_bufs[m].rd_waits() for m in range(DIC)]

            acc2 = LNAcc(DMC, NW)
            for mg in range(2):
                grp, gtok = mm_group(
                    (lambda k, mg=mg: s_wo[mg, k]),
                    16, list(range(mg * G, (mg + 1) * G)),
                    rhs_fn=lambda k: dy[:, k * NW:(k + 1) * NW],
                    rhs_tok_fn=lambda k: y_ready[k],
                    cast_tok=cast["wo"][mg])
                for kk in range(16):
                    dy_bufs[kk].did_read(gtok)
                for j in range(G):
                    m = mg * G + j
                    b = grp[j]
                    t1 = P.emit("s", (lambda e,
                                      o=attn[:, m * NW:(m + 1) * NW],
                                      a=b["ap"][:, 0:NW]:
                                      e.activation(o, a, AF.Identity)),
                                waits=[gtok] + attn_bufs[m].wr_waits())
                    attn_bufs[m].did_write(t1)
                    slot, sbuf_ = sbf.next()
                    t2 = P.emit("s", (lambda e, o=slot[:, 512:1024],
                                      a=b["ap"][:, 0:NW]:
                                      e.activation(o, a, AF.Square)),
                                waits=[gtok] + sbuf_.wr_waits())
                    b["buf"].did_read(t1)
                    b["buf"].did_read(t2)
                    t3 = P.emit("g", (lambda e, o=slot[:, 0:512],
                                      a=attn[:, m * NW:(m + 1) * NW]:
                                      e.tensor_copy(o, a)),
                                waits=[t1] + sbuf_.wr_waits())
                    attn_bufs[m].did_read(t3)
                    sbuf_.w = t2
                    sbuf_.r = [t3]
                    acc2.add(slot, t3, t2, sbuf_)

            brs2, brm2, b1_2, b2_2 = acc2.finish()
            for m in range(DMC):
                slot, sbuf_ = sc32.next()
                a1 = P.emit("v", (lambda e, o=slot[:, 0:NW],
                                  a=attn[:, m * NW:(m + 1) * NW],
                                  b3=brs2["ap"][:, 0:NW]:
                                  e.tensor_tensor(o, a, b3, op=ALU.mult)),
                            waits=[attn_bufs[m].w, b1_2] + sbuf_.wr_waits())
                brs2["buf"].did_read(a1)
                attn_bufs[m].did_read(a1)
                sbuf_.did_write(a1)
                a2 = P.emit("v", (lambda e, o=slot[:, 0:NW], a=slot[:, 0:NW],
                                  b3=brm2["ap"][:, 0:NW]:
                                  e.tensor_tensor(o, a, b3, op=ALU.subtract)),
                            waits=[a1, b2_2])
                brm2["buf"].did_read(a2)
                htoks = []
                for s in range(NSEQ):
                    lo = s * T
                    h1 = P.emit("v", (lambda e,
                                      o=hb[:, m * NW + lo:m * NW + lo + T],
                                      a=slot[:, lo:lo + T],
                                      b3=xw[s][:, m * W + 1:m * W + 1 + T]:
                                      e.tensor_tensor(o, a, b3, op=ALU.add)),
                                waits=[a2] + xw_wtoks[s] +
                                (hb_bufs[m].wr_waits() if s == 0 else []))
                    xw_bufs[s].did_read(h1)
                    htoks.append(h1)
                sbuf_.did_read(htoks[0])
                sbuf_.did_read(htoks[1])
                hb_bufs[m].w = htoks[-1]
                hb_bufs[m].r = [htoks[0]]

            acc3 = LNAcc(DMC, NW)
            for m in range(DMC):
                slot, sbuf_ = sbf.next()
                hw = merge_toks([hb_bufs[m].w] + hb_bufs[m].r)
                t3 = P.emit("g", (lambda e, o=slot[:, 0:512],
                                  a=hb[:, m * NW:(m + 1) * NW]:
                                  e.tensor_copy(o, a)),
                            waits=hw + sbuf_.wr_waits())
                t2 = P.emit("s", (lambda e, o=slot[:, 512:1024],
                                  a=hb[:, m * NW:(m + 1) * NW]:
                                  e.activation(o, a, AF.Square)),
                            waits=hw + sbuf_.wr_waits())
                hb_bufs[m].did_read(t3)
                hb_bufs[m].did_read(t2)
                sbuf_.w = t2
                sbuf_.r = [t3]
                acc3.add(slot, t3, t2, sbuf_)
            brs3, brm3, b1_3, b2_3 = acc3.finish()
            hn_toks = []
            for m in range(DMC):
                slot, sbuf_ = sc32.next()
                a1 = P.emit("v", (lambda e, o=slot[:, 0:NW],
                                  a=hb[:, m * NW:(m + 1) * NW],
                                  b3=brs3["ap"][:, 0:NW]:
                                  e.tensor_tensor(o, a, b3, op=ALU.mult)),
                            waits=[hb_bufs[m].w, b1_3] + sbuf_.wr_waits())
                brs3["buf"].did_read(a1)
                hb_bufs[m].did_read(a1)
                sbuf_.did_write(a1)
                a2 = P.emit("v", (lambda e, o=xh[:, m * NW:(m + 1) * NW],
                                  a=slot[:, 0:NW], b3=brm3["ap"][:, 0:NW]:
                                  e.tensor_tensor(o, a, b3, op=ALU.subtract)),
                            waits=[a1, b2_3] + xh_bufs[m].wr_waits())
                brm3["buf"].did_read(a2)
                sbuf_.did_read(a2)
                xh_bufs[m].did_write(a2)
                hn_toks.append(a2)

            acc4 = LNAcc(DMC, NW)
            for mg in range(2):
                grp, gtok = mm_group(
                    (lambda k, mg=mg: s_wf[mg, k]),
                    8, list(range(mg * G, (mg + 1) * G)),
                    rhs_fn=lambda k: xh[:, k * NW:(k + 1) * NW],
                    rhs_tok_fn=lambda k: [hn_toks[k]],
                    cast_tok=cast["wf"][mg])
                for kk in range(DMC):
                    xh_bufs[kk].did_read(gtok)
                for j in range(G):
                    m = mg * G + j
                    b = grp[j]
                    t1 = P.emit("s", (lambda e,
                                      o=ffb[:, m * NW:(m + 1) * NW],
                                      a=b["ap"][:, 0:NW],
                                      bi=bselS[:, m:m + 1]:
                                      e.activation(o, a, AF.Identity,
                                                   bias=bi)),
                                waits=[gtok, g19] + ff_prolog_reads +
                                ff_bufs[m].wr_waits())
                    ff_bufs[m].did_write(t1)
                    slot, sbuf_ = sbf.next()
                    t2 = P.emit("s", (lambda e, o=slot[:, 512:1024],
                                      a=b["ap"][:, 0:NW],
                                      bi=bselS[:, m:m + 1]:
                                      e.activation(o, a, AF.Square, bias=bi)),
                                waits=[gtok, g19] + sbuf_.wr_waits())
                    b["buf"].did_read(t1)
                    b["buf"].did_read(t2)
                    t3 = P.emit("g", (lambda e, o=slot[:, 0:512],
                                      a=ffb[:, m * NW:(m + 1) * NW]:
                                      e.tensor_copy(o, a)),
                                waits=[t1] + sbuf_.wr_waits())
                    ff_bufs[m].did_read(t3)
                    sbuf_.w = t2
                    sbuf_.r = [t3]
                    acc4.add(slot, t3, t2, sbuf_)

            if it + 1 < NIT:
                head = emit_head(it + 1)
                xn_tok, xn_ctoks = emit_ln1_apply(head)

            pending_tail[0] = (acc4, tk)

        flush_tail()
        P.wait_only("y", out_dma_toks)

        def replay(e, name):
            for ws, fn, tok in P.stream[name]:
                for semn, val in ws:
                    e.wait_ge(sems[semn], val)
                if fn is None:
                    continue
                inst = fn(e)
                if tok is not None:
                    semn = tok[0]
                    inst.then_inc(sems[semn], 16 if semn in DMA_SEMS else 1)

        @block.tensor
        def _(e):
            replay(e, "t")

        @block.vector
        def _(e):
            replay(e, "v")

        @block.scalar
        def _(e):
            replay(e, "s")

        @block.gpsimd
        def _(e):
            replay(e, "g")

        @block.sync
        def _(e):
            replay(e, "y")

    return nc


_NC_CACHE = {}


def _get_nc():
    if "nc" not in _NC_CACHE:
        _NC_CACHE["nc"] = build_program()
    return _NC_CACHE["nc"]


def kernel(L0_attn_x, L0_attn_W_in, L0_attn_conv_w, L0_attn_conv_b,
           L0_attn_W_dt, L0_attn_A_proj, L0_attn_B_proj, L0_attn_C_proj,
           L0_attn_W_out, L0_ffn_W, L0_ffn_b, L0_ffn_logits, L0_ffn_u):
    x = np.asarray(L0_attn_x, dtype=np.float32)
    xt = np.ascontiguousarray(x.transpose(0, 2, 1))

    w_in = np.asarray(L0_attn_W_in, np.float32)
    wu_h = np.ascontiguousarray(w_in[:DI, :].T)
    wz_h = np.ascontiguousarray(w_in[DI:, :].T)
    wdt_h = np.ascontiguousarray(np.asarray(L0_attn_W_dt, np.float32).T)
    wap_h = np.ascontiguousarray(np.asarray(L0_attn_A_proj, np.float32).T)
    wbp_h = np.ascontiguousarray(np.asarray(L0_attn_B_proj, np.float32).T)
    wcp_h = np.ascontiguousarray(np.asarray(L0_attn_C_proj, np.float32))
    wo_h = np.ascontiguousarray(np.asarray(L0_attn_W_out, np.float32).T)
    wff_h = np.ascontiguousarray(np.asarray(L0_ffn_W, np.float32))
    bff_h = np.ascontiguousarray(np.asarray(L0_ffn_b, np.float32))
    cw_h = np.ascontiguousarray(
        np.asarray(L0_attn_conv_w, np.float32).reshape(16, 128, 3)
        .transpose(1, 0, 2).reshape(128, 48))
    cb_h = np.ascontiguousarray(
        np.asarray(L0_attn_conv_b, np.float32).reshape(16, 128).T)
    lg_h = np.ascontiguousarray(np.asarray(L0_ffn_logits, np.float32))
    uu_h = np.ascontiguousarray(np.asarray(L0_ffn_u, np.float32))

    nc = _get_nc()
    in_maps = []
    for c in range(NCORES):
        in_maps.append({
            "x": xt[c * NSEQ:(c + 1) * NSEQ],
            "wu": wu_h, "wz": wz_h, "wdt": wdt_h, "wap": wap_h,
            "wbp": wbp_h, "wcp": wcp_h, "wo": wo_h, "wff": wff_h,
            "bff": bff_h, "cw": cw_h, "cb": cb_h, "lg8": lg_h, "uu8": uu_h,
        })
    res = run_bass_kernel_spmd(nc, in_maps, core_ids=list(range(NCORES)))
    parts = [r["o"] for r in res.results]
    out = np.concatenate(parts, axis=0)
    return np.ascontiguousarray(out.transpose(0, 2, 1)).astype(np.float32)


# version 20 (final)
# speedup vs baseline: 1.2046x; 1.2046x over previous
import numpy as np
import concourse.bass as bass
import concourse.mybir as mybir
from concourse.bass_utils import run_bass_kernel_spmd
from contextlib import ExitStack

F32 = mybir.dt.float32
BF16 = mybir.dt.bfloat16
AF = mybir.ActivationFunctionType
ALU = mybir.AluOpType
AXX = mybir.AxisListType.X

NCORES = 8
NSEQ = 2
L = 2048
DM = 1024
DI = 2048
T = 256
NIT = L // T
W = T + 2
DMC = DM // 128
DIC = DI // 128
NW = NSEQ * T
G = 4
EPS = 1e-5
NST = 10
NSB = 4
NSTREAM = 8


def merge_toks(toks):
    best = {}
    for t in toks:
        if t is None:
            continue
        sem, val = t
        if best.get(sem, 0) < val:
            best[sem] = val
    return [(s, v) for s, v in best.items()]


class Prog:
    ENG = ("t", "v", "s", "g", "y")

    def __init__(self):
        self.stream = {e: [] for e in self.ENG}
        self.cnt = {e: 0 for e in self.ENG}
        self.dcnt = {}
        self.waited = {e: {} for e in self.ENG}

    def emit(self, eng, fn, waits=(), sig=True, dma=None):
        ws = []
        wd = self.waited[eng]
        for tok in merge_toks(waits):
            sem, val = tok
            if wd.get(sem, 0) >= val:
                continue
            wd[sem] = val
            ws.append((sem, val))
        if dma is not None:
            self.dcnt[dma] = self.dcnt.get(dma, 0) + 16
            tok = (dma, self.dcnt[dma])
        elif sig:
            self.cnt[eng] += 1
            tok = (eng, self.cnt[eng])
        else:
            tok = None
        self.stream[eng].append((ws, fn, tok))
        return tok

    def wait_only(self, eng, waits):
        self.emit(eng, None, waits=waits, sig=False)


class Buf:
    def __init__(self):
        self.w = None
        self.r = []

    def wr_waits(self):
        return merge_toks([self.w] + self.r)

    def rd_waits(self):
        return [self.w] if self.w is not None else []

    def did_write(self, tok):
        self.w = tok
        self.r = []

    def did_read(self, tok):
        if tok is not None:
            self.r.append(tok)


class Slots:
    def __init__(self, ap, n, width):
        self.ap = ap
        self.n = n
        self.width = width
        self.bufs = [Buf() for _ in range(n)]
        self.i = 0

    def next(self):
        i = self.i % self.n
        self.i += 1
        return self.ap[:, i * self.width:(i + 1) * self.width], self.bufs[i]


def build_program():
    nc = bass.Bass("TRN2", target_bir_lowering=False)
    P = Prog()

    xin = nc.dram_tensor("x", [NSEQ, DM, L], F32, kind="ExternalInput").ap()
    wu = nc.dram_tensor("wu", [DM, DI], F32, kind="ExternalInput").ap()
    wz = nc.dram_tensor("wz", [DM, DI], F32, kind="ExternalInput").ap()
    wdt = nc.dram_tensor("wdt", [DI, DI], F32, kind="ExternalInput").ap()
    wap = nc.dram_tensor("wap", [DI, DI], F32, kind="ExternalInput").ap()
    wbp = nc.dram_tensor("wbp", [DI, DI], F32, kind="ExternalInput").ap()
    wcp = nc.dram_tensor("wcp", [DI, DI], F32, kind="ExternalInput").ap()
    wo = nc.dram_tensor("wo", [DI, DM], F32, kind="ExternalInput").ap()
    wff = nc.dram_tensor("wff", [8, DM, DM], F32, kind="ExternalInput").ap()
    bff = nc.dram_tensor("bff", [8, DM], F32, kind="ExternalInput").ap()
    cw = nc.dram_tensor("cw", [128, 48], F32, kind="ExternalInput").ap()
    cb = nc.dram_tensor("cb", [128, 16], F32, kind="ExternalInput").ap()
    lg8 = nc.dram_tensor("lg8", [8], F32, kind="ExternalInput").ap()
    uu8 = nc.dram_tensor("uu8", [8], F32, kind="ExternalInput").ap()
    oout = nc.dram_tensor("o", [NSEQ, DM, L], F32, kind="ExternalOutput").ap()

    s_wu = nc.dram_tensor("s_wu", [4, 8, 128, 512], BF16, kind="Internal").ap()
    s_wz = nc.dram_tensor("s_wz", [4, 8, 128, 512], BF16, kind="Internal").ap()
    s_wdt = nc.dram_tensor("s_wdt", [4, 16, 128, 512], BF16, kind="Internal").ap()
    s_wa = nc.dram_tensor("s_wa", [4, 16, 128, 512], BF16, kind="Internal").ap()
    s_wb = nc.dram_tensor("s_wb", [4, 16, 128, 512], BF16, kind="Internal").ap()
    s_wc = nc.dram_tensor("s_wc", [4, 16, 128, 512], BF16, kind="Internal").ap()
    s_wo = nc.dram_tensor("s_wo", [2, 16, 128, 512], BF16, kind="Internal").ap()
    s_wf = nc.dram_tensor("s_wf", [2, 8, 128, 512], BF16, kind="Internal").ap()

    with ExitStack() as ctx:
        def sb(name, shape, dt):
            return ctx.enter_context(nc.sbuf_tensor(name, shape, dt))

        xw = [sb("xw0", [128, DMC * W], F32), sb("xw1", [128, DMC * W], F32)]
        xnb = [sb("xn0", [128, DMC * W], BF16), sb("xn1", [128, DMC * W], BF16)]
        u_bf = sb("u_bf", [128, DIC * NW], BF16)
        dy = sb("dy", [128, DIC * NW], BF16)
        xh = sb("xh", [128, DIC * NW], BF16)
        dsb = sb("dsb", [128, DIC * NW], BF16)
        attn = sb("attn", [128, DMC * NW], F32)
        hb = sb("hb", [128, DMC * NW], F32)
        ffb = sb("ffb", [128, DMC * NW], F32)
        sc32 = Slots(sb("sc32", [128, NST * NW], F32), NST, NW)
        sbf = Slots(sb("sbf", [128, NSB * 1024], BF16), NSB, 1024)
        stb = sb("stb", [128, NSTREAM * 512], BF16)
        st_bufs = [Buf() for _ in range(NSTREAM)]
        pA_in = sb("pA_in", [128, 2048], F32)
        pA_out = sb("pA_out", [128, 2048], BF16)
        psel = sb("psel", [128, 2 * 1024], BF16)
        ones_sc = sb("ones_sc", [128, 1], BF16)
        ones_row = sb("ones_row", [1, 128], BF16)
        onesT = sb("onesT", [128, NW], F32)
        cwsb = sb("cwsb", [128, 48], F32)
        cbsb = sb("cbsb", [128, 16], F32)
        xst = [sb("xst0", [128, DIC], F32), sb("xst1", [128, DIC], F32)]
        dst = [sb("dst0", [128, DIC], F32), sb("dst1", [128, DIC], F32)]
        bselS = sb("bselS", [128, 8], F32)
        iota8 = sb("iota8", [1, 8], F32)
        idxf = sb("idxf", [128, 1], F32)
        iota2d = sb("iota2d", [128, 8], F32)
        idxTf = sb("idxTf", [128, 8], F32)
        idxT = sb("idxT", [128, 8], mybir.dt.int32)
        epsC = sb("epsC", [1, 1], F32)
        meanS = sb("meanS", [1, NW], F32)
        tmpA = sb("tmpA", [1, NW], F32)
        tmpB = sb("tmpB", [1, NW], F32)
        rstdS = sb("rstdS", [1, NW], BF16)
        mrS = sb("mrS", [1, NW], BF16)
        t8a = sb("t8a", [1, 8], F32)
        t8b = sb("t8b", [1, 8], F32)
        t8c = sb("t8c", [1, 8], F32)
        t8m = sb("t8m", [1, 1], F32)
        t8n = sb("t8n", [1, 1], F32)
        t8oh = sb("t8oh", [1, 8], F32)
        t8ohb = sb("t8ohb", [1, 8], BF16)

        banks = []
        for i in range(8):
            ap = ctx.enter_context(nc.psum_tensor(f"bank{i}", [128, 512], F32))
            banks.append({"ap": ap, "buf": Buf(), "held": False})
        bank_ptr = [0]

        def take_banks(n):
            out = []
            spins = 0
            while len(out) < n:
                b = banks[bank_ptr[0] % 8]
                bank_ptr[0] += 1
                spins += 1
                assert spins < 64, "bank pool exhausted (too many held banks)"
                if b["held"] or b in out:
                    continue
                out.append(b)
            return out

        DMA_SEMS = (["sty%d" % i for i in range(NSTREAM)] +
                    ["stg%d" % i for i in range(NSTREAM)] +
                    ["dx0", "dx1", "do", "wai", "wao0", "wao1",
                     "xp0", "xp1", "wfo", "wfo1", "dc", "dr"] +
                    ["c_wu%d" % i for i in range(4)] +
                    ["c_wdt%d" % i for i in range(4)] +
                    ["c_wb%d" % i for i in range(4)] +
                    ["c_wc%d" % i for i in range(4)] +
                    ["c_wz%d" % i for i in range(4)] +
                    ["c_wo%d" % i for i in range(2)])
        sems = {}
        for nm in ["t", "v", "s", "g"] + DMA_SEMS:
            sems[nm] = ctx.enter_context(nc.semaphore(nm))
        DMA_SEMS = set(DMA_SEMS)
        block = ctx.enter_context(nc.Block())

        def dma(queue, out_ap, in_ap, waits=(), sem=None):
            return P.emit(queue, (lambda e, o=out_ap, i=in_ap: e.dma_start(o, i)),
                          waits=waits, dma=sem)

        cur_it = [0]
        stream_q = [0]

        def load_st(src_ap, waits=()):
            i = stream_q[0] % NSTREAM
            stream_q[0] += 1
            slot = st_bufs[i]
            ap = stb[:, i * 512:(i + 1) * 512]
            tok = P.emit("y", (lambda e, o=ap, s2=src_ap: e.dma_start(o, s2)),
                         waits=list(waits) + slot.wr_waits(),
                         dma="sty%d" % i)
            slot.did_write(tok)
            return ap, tok, slot

        def mm_group(scratch_blk, K, mslots, rhs_fn, rhs_tok_fn, cast_tok,
                     out_lo=0, out_hi=NW):
            ng = len(mslots)
            grp = take_banks(ng)
            gtok = None
            cast_w = (list(cast_tok) if isinstance(cast_tok, list)
                      else [cast_tok])
            for k in range(K):
                ap, tok, slot = load_st(scratch_blk(k), waits=cast_w)
                waits = [tok] + (rhs_tok_fn(k) if rhs_tok_fn else [])
                rhs = rhs_fn(k)
                for j in range(ng):
                    b = grp[j]
                    last = (j == ng - 1)
                    w = list(waits) if j == 0 else []
                    if k == 0:
                        w += b["buf"].wr_waits()
                    kt = P.emit("t", (lambda e, o=b["ap"][:, out_lo:out_hi],
                                      lh=ap[:, j * 128:(j + 1) * 128], r=rhs,
                                      st=(k == 0), sp=(k == K - 1):
                                      e.matmul(o, lh, r, start=st, stop=sp)),
                                waits=w, sig=last)
                    if last:
                        slot.did_read(kt)
                        gtok = kt
            for b in grp:
                b["buf"].did_write(gtok)
            return grp, gtok

        chainB = {nm: Buf() for nm in ("mean", "tmpA", "tmpB", "rstd", "mr")}

        class LNAcc:
            def __init__(self, n, width):
                self.n = n
                self.width = width
                self.c = 0
                self.bsum, self.bsq = take_banks(2)
                self.bsum["held"] = True
                self.bsq["held"] = True
                self.last = [None, None]

            def add(self, slot, t_data, t_sq, sbuf_):
                c, n, width = self.c, self.n, self.width
                m1 = P.emit("t", (lambda e, o=self.bsum["ap"][0:1, 0:width],
                                  r=slot[:, 0:width], st=(c == 0),
                                  sp=(c == n - 1):
                                  e.matmul(o, ones_sc[:, 0:1], r,
                                           start=st, stop=sp)),
                            waits=[t_data, k0] +
                            (self.bsum["buf"].wr_waits() if c == 0 else []))
                m2 = P.emit("t", (lambda e, o=self.bsq["ap"][0:1, 0:width],
                                  r=slot[:, 512:512 + width], st=(c == 0),
                                  sp=(c == n - 1):
                                  e.matmul(o, ones_sc[:, 0:1], r,
                                           start=st, stop=sp)),
                            waits=[t_sq] +
                            (self.bsq["buf"].wr_waits() if c == 0 else []))
                sbuf_.did_read(m1)
                sbuf_.did_read(m2)
                self.last = [m1, m2]
                self.c += 1

            def finish(self):
                width = self.width
                bsum, bsq = self.bsum, self.bsq
                bsum["held"] = False
                bsq["held"] = False
                bsum["buf"].did_write(self.last[0])
                bsq["buf"].did_write(self.last[1])
                c1 = P.emit("v", (lambda e: e.tensor_copy(
                    meanS[0:1, 0:width], bsum["ap"][0:1, 0:width])),
                    waits=[self.last[0]] + chainB["mean"].wr_waits())
                bsum["buf"].did_read(c1)
                chainB["mean"].did_write(c1)
                c2 = P.emit("s", (lambda e: e.activation(
                    tmpA[0:1, 0:width], meanS[0:1, 0:width], AF.Square)),
                    waits=[c1] + chainB["tmpA"].wr_waits())
                chainB["mean"].did_read(c2)
                chainB["tmpA"].did_write(c2)
                c3 = P.emit("v", (lambda e: e.tensor_tensor(
                    tmpB[0:1, 0:width], bsq["ap"][0:1, 0:width],
                    tmpA[0:1, 0:width], op=ALU.subtract)),
                    waits=[self.last[1], c2] + chainB["tmpB"].wr_waits())
                bsq["buf"].did_read(c3)
                chainB["tmpA"].did_read(c3)
                chainB["tmpB"].did_write(c3)
                c4 = P.emit("s", (lambda e: e.activation(
                    tmpA[0:1, 0:width], tmpB[0:1, 0:width], AF.Ln,
                    bias=epsC[0:1, 0:1])),
                    waits=[c3, kE] + chainB["tmpA"].wr_waits())
                chainB["tmpB"].did_read(c4)
                chainB["tmpA"].did_write(c4)
                c5 = P.emit("s", (lambda e: e.activation(
                    rstdS[0:1, 0:width], tmpA[0:1, 0:width], AF.Exp,
                    scale=-0.5)),
                    waits=[c4] + chainB["rstd"].wr_waits())
                chainB["tmpA"].did_read(c5)
                chainB["rstd"].did_write(c5)
                c6 = P.emit("v", (lambda e: e.tensor_tensor(
                    mrS[0:1, 0:width], meanS[0:1, 0:width],
                    rstdS[0:1, 0:width], op=ALU.mult)),
                    waits=[c5] + chainB["mr"].wr_waits())
                chainB["mean"].did_read(c6)
                chainB["rstd"].did_read(c6)
                chainB["mr"].did_write(c6)
                brs, brm = take_banks(2)
                b1 = P.emit("t", (lambda e: e.matmul(
                    brs["ap"][:, 0:width], ones_row[0:1, 0:128],
                    rstdS[0:1, 0:width], start=True, stop=True)),
                    waits=[c5, k1] + brs["buf"].wr_waits())
                chainB["rstd"].did_read(b1)
                b2 = P.emit("t", (lambda e: e.matmul(
                    brm["ap"][:, 0:width], ones_row[0:1, 0:128],
                    mrS[0:1, 0:width], start=True, stop=True)),
                    waits=[c6] + brm["buf"].wr_waits())
                chainB["mr"].did_read(b2)
                brs["buf"].did_write(b1)
                brm["buf"].did_write(b2)
                return brs, brm, b1, b2

        k0 = P.emit("v", lambda e: e.memset(ones_sc[:, :], 1.0 / DM))
        k1 = P.emit("v", lambda e: e.memset(ones_row[:, :], 1.0))
        k2 = P.emit("v", lambda e: e.memset(onesT[:, :], 1.0))
        kE = P.emit("v", lambda e: e.memset(epsC[:, :], float(EPS)))
        dma("g", cwsb[:, :], cw, sem="dc")
        dma("g", cbsb[:, :], cb, sem="dc")
        dma("g", t8a[:, :], lg8.rearrange("(a b) -> a b", a=1), sem="dc")
        tok_dc = dma("g", t8b[:, :], uu8.rearrange("(a b) -> a b", a=1),
                     sem="dc")
        tcw = tcb = tlg = tuu = tok_dc

        cast = {}

        def cast_stage(name, src, dstap, n_mg):
            n_k = dstap.shape[1]
            view = src.rearrange("(k p) (mg q) -> k mg p q", p=128, q=512)
            toks = [dma("g", dstap[:, k], view[k],
                        sem="c_%s%d" % (name, k % n_mg))
                    for k in range(n_k)]
            cast[name] = [toks] * n_mg

        g1 = P.emit("s", lambda e: e.activation(t8c[:, :], t8b[:, :], AF.Ln),
                    waits=[tuu])
        g2 = P.emit("s", lambda e: e.activation(t8b[:, :], t8c[:, :], AF.Ln,
                                                scale=-1.0), waits=[g1])
        g3 = P.emit("v", lambda e: e.tensor_reduce(t8m[:, :], t8a[:, :],
                                                   axis=AXX, op=ALU.max),
                    waits=[tlg])
        g4 = P.emit("v", lambda e: e.tensor_single_scalar(t8c[:, :], t8a[:, :],
                                                          t8m[0:1, 0:1],
                                                          op=ALU.subtract),
                    waits=[g3, g2])
        g5 = P.emit("s", lambda e: e.activation(t8oh[:, :], t8c[:, :], AF.Exp),
                    waits=[g4])
        g6 = P.emit("v", lambda e: e.tensor_reduce(t8n[:, :], t8oh[:, :],
                                                   axis=AXX, op=ALU.add),
                    waits=[g5])
        g7 = P.emit("s", lambda e: e.activation(t8n[:, :], t8n[:, :], AF.Ln),
                    waits=[g6])
        g8 = P.emit("v", lambda e: e.tensor_tensor(t8c[:, :], t8a[:, :],
                                                   t8b[:, :], op=ALU.subtract),
                    waits=[g2, g5, tlg])
        g9 = P.emit("v", lambda e: e.tensor_single_scalar(t8c[:, :], t8c[:, :],
                                                          t8n[0:1, 0:1],
                                                          op=ALU.subtract),
                    waits=[g7, g8])
        g11 = P.emit("v", lambda e: e.tensor_single_scalar(t8c[:, :],
                                                           t8c[:, :],
                                                           t8m[0:1, 0:1],
                                                           op=ALU.subtract),
                     waits=[g9])
        g12 = P.emit("v", lambda e: e.tensor_reduce(t8m[:, :], t8c[:, :],
                                                    axis=AXX, op=ALU.max),
                     waits=[g11])
        g13 = P.emit("v", lambda e: e.tensor_single_scalar(t8oh[:, :],
                                                           t8c[:, :],
                                                           t8m[0:1, 0:1],
                                                           op=ALU.is_ge),
                     waits=[g12, g6])
        gi8 = P.emit("g", lambda e: e.iota(iota8[:, :], [[1, 8]],
                                           channel_multiplier=0,
                                           allow_small_or_imprecise_dtypes=True))
        gi2 = P.emit("g", lambda e: e.iota(iota2d[:, :], [[128, 8]],
                                           channel_multiplier=1,
                                           allow_small_or_imprecise_dtypes=True))
        g14 = P.emit("v", lambda e: e.tensor_tensor(t8c[:, :], t8oh[:, :],
                                                    iota8[:, :], op=ALU.mult),
                     waits=[g13, gi8])
        g15 = P.emit("v", lambda e: e.tensor_reduce(t8m[:, :], t8c[:, :],
                                                    axis=AXX, op=ALU.add),
                     waits=[g14])
        g16 = P.emit("v", lambda e: e.tensor_single_scalar(t8ohb[0:1, 0:1],
                                                           t8m[:, :], 1024.0,
                                                           op=ALU.mult),
                     waits=[g15])
        boh = take_banks(1)[0]
        g17 = P.emit("t", lambda e: e.matmul(boh["ap"][:, 0:1],
                                             ones_row[0:1, 0:128],
                                             t8ohb[0:1, 0:1],
                                             start=True, stop=True),
                     waits=[g16, k1] + boh["buf"].wr_waits())
        boh["buf"].did_write(g17)
        g18 = P.emit("v", lambda e: e.tensor_copy(idxf[:, :],
                                                  boh["ap"][:, 0:1]),
                     waits=[g17])
        boh["buf"].did_read(g18)
        g18b = P.emit("v", lambda e: e.tensor_scalar(idxTf[:, :], iota2d[:, :],
                                                     idxf[:, 0:1], None,
                                                     op0=ALU.add),
                      waits=[g18, gi2])
        g19i = P.emit("v", lambda e: e.tensor_copy(idxT[:, :], idxTf[:, :]),
                      waits=[g18b])
        bffv = bff.rearrange("a (b c) -> (a b) c", c=1)
        bsel_toks = []
        for m in range(8):
            tm = P.emit("g", (lambda e, m=m: e.indirect_dma_start(
                bselS[:, m:m + 1], None, bffv,
                bass.IndirectOffsetOnAxis(idxT[:, m:m + 1], 0))),
                waits=[g19i], dma="xp0")
            bsel_toks.append(tm)
        g19 = bsel_toks[-1]
        wffv = wff.rearrange("e d c -> (e d) c")
        pselB = [Buf(), Buf()]
        ffslotB = [Buf(), Buf()]
        wf_last = None
        for kd in range(8):
            sl = kd % 2
            tile = ffb[:, sl * 1024:(sl + 1) * 1024]
            tin = P.emit("g", (lambda e, t2=tile, kd=kd: e.indirect_dma_start(
                t2, None, wffv,
                bass.IndirectOffsetOnAxis(idxT[:, kd:kd + 1], 0))),
                waits=[g19i] + ffslotB[sl].wr_waits(),
                dma=("xp1" if sl == 0 else "dr"))
            ffslotB[sl].did_write(tin)
            outt = psel[:, sl * 1024:(sl + 1) * 1024]
            tc = P.emit("s", (lambda e, o=outt, a=tile:
                              e.activation(o, a, AF.Identity)),
                        waits=[tin] + pselB[sl].wr_waits())
            ffslotB[sl].did_read(tc)
            pselB[sl].did_write(tc)
            dstv = s_wf[0:2, kd].rearrange("mg p q -> p mg q")
            tw = P.emit("y", (lambda e, o=dstv,
                              a=outt.rearrange("p (mg q) -> p mg q", q=512):
                              e.dma_start(o, a)),
                        waits=[tc], dma=("wfo" if sl == 0 else "wfo1"))
            pselB[sl].did_read(tw)
        wf_done = [("wfo", P.dcnt["wfo"]), ("wfo1", P.dcnt["wfo1"])]
        cast["wf"] = [wf_done, wf_done]
        ff_prolog_reads = list(wf_done)

        cast_stage("wu", wu, s_wu, 4)

        expa_state = {"in": Buf(), "out": Buf(), "k": 0, "toks": []}

        def emit_expa_piece():
            k = expa_state["k"]
            if k >= 16:
                return
            expa_state["k"] += 1
            tin = P.emit("s", (lambda e, o=pA_in[:, :],
                               s2=wap[k * 128:(k + 1) * 128, :]:
                               e.dma_start(o, s2)),
                         waits=expa_state["in"].wr_waits(), dma="wai")
            expa_state["in"].did_write(tin)
            te = P.emit("s", (lambda e, o=pA_out[:, :], a=pA_in[:, :]:
                              e.activation(o, a, AF.Exp)),
                        waits=[tin] + expa_state["out"].wr_waits())
            expa_state["in"].did_read(te)
            expa_state["out"].did_write(te)
            tw = P.emit("s", (lambda e,
                              o=s_wa[:, k].rearrange("mg p q -> p mg q"),
                              a=pA_out[:, :].rearrange("p (mg q) -> p mg q",
                                                       q=512):
                              e.dma_start(o, a)),
                        waits=[te], dma="wao0")
            expa_state["out"].did_read(tw)
            expa_state["toks"].append(tw)
        cast["wa"] = None

        xw_bufs = [Buf(), Buf()]
        xn_bufs = [Buf(), Buf()]
        ubf_bufs = [Buf() for _ in range(DIC)]
        dy_bufs = [Buf() for _ in range(DIC)]
        xh_bufs = [Buf() for _ in range(DIC)]
        ds_bufs = [Buf() for _ in range(DIC)]
        attn_bufs = [Buf() for _ in range(DMC)]
        hb_bufs = [Buf() for _ in range(DMC)]
        ff_bufs = [Buf() for _ in range(DMC)]
        st_state = [[Buf() for _ in range(DIC)], [Buf() for _ in range(DIC)]]
        ds_state = [[Buf() for _ in range(DIC)], [Buf() for _ in range(DIC)]]
        out_dma_toks = []

        xview = [xin[s].rearrange("(c p) t -> p c t", p=128)
                 for s in range(NSEQ)]
        oview = [oout[s].rearrange("(c p) t -> p c t", p=128)
                 for s in range(NSEQ)]

        def emit_head(it):
            tk = it * T
            xw_wtoks = [[], []]
            for s in range(NSEQ):
                xwv = xw[s][:, :].rearrange("p (c w) -> p c w", w=W)
                war = xw_bufs[s].wr_waits()
                if it == 0:
                    t1 = dma("y", xwv[:, :, 1:W], xview[s][:, :, 0:W - 1],
                             waits=war, sem="dx%d" % s)
                    t2 = P.emit("v", (lambda e, o=xwv[:, :, 0:1]:
                                      e.memset(o, 0.0)), waits=war)
                    xw_wtoks[s] = [t1, t2]
                elif it == NIT - 1:
                    t1 = dma("y", xwv[:, :, 0:W - 1],
                             xview[s][:, :, tk - 1:L], waits=war,
                             sem="dx%d" % s)
                    t2 = P.emit("v", (lambda e, o=xwv[:, :, W - 1:W]:
                                      e.memset(o, 0.0)), waits=war)
                    xw_wtoks[s] = [t1, t2]
                else:
                    t1 = dma("y", xwv[:, :, :],
                             xview[s][:, :, tk - 1:tk + W - 1], waits=war,
                             sem="dx%d" % s)
                    xw_wtoks[s] = [t1]
                xw_bufs[s].w = xw_wtoks[s][0]
                xw_bufs[s].r = list(xw_wtoks[s][1:])
            heads = []
            for s in range(NSEQ):
                acc1 = LNAcc(DMC, W)
                for c in range(DMC):
                    slot, sbuf_ = sbf.next()
                    tq = P.emit("g", (lambda e, o=slot[:, 0:W],
                                      a=xw[s][:, c * W:(c + 1) * W]:
                                      e.tensor_copy(o, a)),
                                waits=xw_wtoks[s] + sbuf_.wr_waits())
                    tsq = P.emit("v", (lambda e, o=slot[:, 512:512 + W],
                                       a=xw[s][:, c * W:(c + 1) * W]:
                                       e.tensor_tensor(o, a, a, op=ALU.mult)),
                                 waits=xw_wtoks[s] + sbuf_.wr_waits())
                    xw_bufs[s].did_read(tq)
                    xw_bufs[s].did_read(tsq)
                    sbuf_.w = tsq
                    sbuf_.r = [tq]
                    acc1.add(slot, tq, tsq, sbuf_)
                brs, brm, b1, b2 = acc1.finish()
                brs["held"] = True
                brm["held"] = True
                heads.append((brs, brm, b1, b2))
            return {"xw_wtoks": xw_wtoks, "heads": heads}

        def emit_ln1_apply(head):
            xw_wtoks = head["xw_wtoks"]
            xn_tok = [None, None]
            for s in range(NSEQ):
                brs, brm, b1, b2 = head["heads"][s]
                brs["held"] = False
                brm["held"] = False
                atoks = []
                for c in range(DMC):
                    slot, sbuf_ = sc32.next()
                    a1 = P.emit("v", (lambda e, o=slot[:, 0:W],
                                      a=xw[s][:, c * W:(c + 1) * W],
                                      b3=brs["ap"][:, 0:W]:
                                      e.tensor_tensor(o, a, b3, op=ALU.mult)),
                                waits=xw_wtoks[s] + [b1] + sbuf_.wr_waits())
                    brs["buf"].did_read(a1)
                    xw_bufs[s].did_read(a1)
                    sbuf_.did_write(a1)
                    a2 = P.emit("v", (lambda e,
                                      o=xnb[s][:, c * W:(c + 1) * W],
                                      a=slot[:, 0:W], b3=brm["ap"][:, 0:W]:
                                      e.tensor_tensor(o, a, b3,
                                                      op=ALU.subtract)),
                                waits=[a1, b2] +
                                (xn_bufs[s].wr_waits() if c == 0 else []))
                    brm["buf"].did_read(a2)
                    sbuf_.did_read(a2)
                    atoks.append(a2)
                xn_bufs[s].did_write(atoks[-1])
                xn_tok[s] = atoks[-1]
            return xn_tok

        head = emit_head(0)
        xn_tok = emit_ln1_apply(head)
        cast_stage("wdt", wdt, s_wdt, 4)

        for it in range(NIT):
            cur_it[0] = it
            tk = it * T
            xw_wtoks = head["xw_wtoks"]

            for s in range(NSEQ):
                for mg in range(4):
                    grp, gtok = mm_group(
                        (lambda k, mg=mg: s_wu[mg, k]),
                        8, list(range(mg * G, (mg + 1) * G)),
                        rhs_fn=lambda k: xnb[s][:, k * W:(k + 1) * W],
                        rhs_tok_fn=lambda k: [xn_tok[s]],
                        cast_tok=cast["wu"][mg], out_lo=0, out_hi=W)
                    xn_bufs[s].did_read(gtok)
                    for j in range(G):
                        m = mg * G + j
                        b = grp[j]
                        slot, sbuf_ = sc32.next()
                        acc = slot[:, 0:T]
                        v1 = P.emit("v", (lambda e, o=acc,
                                          a=b["ap"][:, 0:T],
                                          w2=cwsb[:, 3 * m:3 * m + 1]:
                                          e.tensor_scalar(o, a, w2, None,
                                                          op0=ALU.mult)),
                                    waits=[gtok, tcw] + sbuf_.wr_waits())
                        v2 = P.emit("v", (lambda e, o=acc,
                                          a=b["ap"][:, 1:T + 1],
                                          w2=cwsb[:, 3 * m + 1:3 * m + 2]:
                                          e.scalar_tensor_tensor(
                                              o, a, w2, o, op0=ALU.mult,
                                              op1=ALU.add)), waits=[v1])
                        v3 = P.emit("v", (lambda e, o=acc,
                                          a=b["ap"][:, 2:T + 2],
                                          w2=cwsb[:, 3 * m + 2:3 * m + 3]:
                                          e.scalar_tensor_tensor(
                                              o, a, w2, o, op0=ALU.mult,
                                              op1=ALU.add)), waits=[v2])
                        b["buf"].did_read(v3)
                        sbuf_.did_write(v3)
                        ucol = m * NW + s * T
                        s1 = P.emit("s", (lambda e, o=u_bf[:, ucol:ucol + T],
                                          a=acc, bi=cbsb[:, m:m + 1]:
                                          e.activation(o, a, AF.Silu,
                                                       bias=bi)),
                                    waits=[v3, tcb] +
                                    (ubf_bufs[m].wr_waits() if s == 0
                                     else []))
                        sbuf_.did_read(s1)
                        if s == 0:
                            ubf_bufs[m].did_write(s1)
                        else:
                            ubf_bufs[m].r.append(s1)
                    emit_expa_piece()
                    emit_expa_piece()
            if cast["wa"] is None:
                cast["wa"] = [[("wao0", P.dcnt["wao0"])]] * 4
                cast_stage("wb", wbp, s_wb, 4)
                cast_stage("wc", wcp, s_wc, 4)
            u_ready = [merge_toks([ubf_bufs[m].w] + ubf_bufs[m].r)
                       for m in range(DIC)]

            for mg in range(4):
                grp, gtok = mm_group(
                    (lambda k, mg=mg: s_wdt[mg, k]),
                    16, list(range(mg * G, (mg + 1) * G)),
                    rhs_fn=lambda k: u_bf[:, k * NW:(k + 1) * NW],
                    rhs_tok_fn=lambda k: u_ready[k],
                    cast_tok=cast["wdt"][mg])
                for kk in range(16):
                    ubf_bufs[kk].did_read(gtok)
                for j in range(G):
                    m = mg * G + j
                    b = grp[j]
                    slot, sbuf_ = sc32.next()
                    e1 = P.emit("s", (lambda e, o=slot[:, 0:NW],
                                      a=b["ap"][:, 0:NW]:
                                      e.activation(o, a, AF.Exp)),
                                waits=[gtok] + sbuf_.wr_waits())
                    b["buf"].did_read(e1)
                    sbuf_.did_write(e1)
                    e2 = P.emit("s", (lambda e, o=dy[:, m * NW:(m + 1) * NW],
                                      a=slot[:, 0:NW]:
                                      e.activation(o, a, AF.Ln, bias=1.0)),
                                waits=[e1] + dy_bufs[m].wr_waits())
                    sbuf_.did_read(e2)
                    dy_bufs[m].did_write(e2)
            if it == 0:
                cast_stage("wz", wz, s_wz, 4)
                cast_stage("wo", wo, s_wo, 2)
            delta_ready = [dy_bufs[m].rd_waits() for m in range(DIC)]

            for mg in range(4):
                ms = list(range(mg * G, (mg + 1) * G))
                grp_a, gtok_a = mm_group(
                    (lambda k, mg=mg: s_wa[mg, k]),
                    16, ms,
                    rhs_fn=lambda k: dy[:, k * NW:(k + 1) * NW],
                    rhs_tok_fn=lambda k: delta_ready[k],
                    cast_tok=cast["wa"][mg])
                grp_b, gtok_b = mm_group(
                    (lambda k, mg=mg: s_wb[mg, k]),
                    16, ms,
                    rhs_fn=lambda k: dy[:, k * NW:(k + 1) * NW],
                    rhs_tok_fn=lambda k: delta_ready[k],
                    cast_tok=cast["wb"][mg])
                for kk in range(16):
                    dy_bufs[kk].did_read(gtok_a)
                    dy_bufs[kk].did_read(gtok_b)
                xs_info = []
                for j, m in enumerate(ms):
                    ba, bb = grp_a[j], grp_b[j]
                    slA, bufA = sc32.next()
                    dA = slA[:, 0:NW]
                    e1 = P.emit("s", (lambda e, o=dA, a=ba["ap"][:, 0:NW]:
                                      e.activation(o, a, AF.Exp, scale=-1.0)),
                                waits=[gtok_a] + bufA.wr_waits())
                    ba["buf"].did_read(e1)
                    bufA.did_write(e1)
                    slB, bufB = sc32.next()
                    dBu = slB[:, 0:NW]
                    ucol = m * NW
                    e2 = P.emit("v", (lambda e, o=dBu, a=bb["ap"][:, 0:NW],
                                      b3=u_bf[:, ucol:ucol + NW]:
                                      e.tensor_tensor(o, a, b3, op=ALU.mult)),
                                waits=[gtok_b] + u_ready[m] + bufB.wr_waits())
                    bb["buf"].did_read(e2)
                    ubf_bufs[m].did_read(e2)
                    bufB.did_write(e2)
                    slX, bufX = sc32.next()
                    xs_ = slX[:, 0:NW]
                    sc_toks = []
                    for s in range(NSEQ):
                        lo, hi = s * T, (s + 1) * T
                        if it == 0:
                            init, iw = 0.0, []
                        else:
                            init = xst[s][:, m:m + 1]
                            iw = st_state[s][m].rd_waits()
                        sct = P.emit("v", (lambda e, o=xs_[:, lo:hi],
                                           a=dA[:, lo:hi], b3=dBu[:, lo:hi],
                                           ii=init:
                                           e.tensor_tensor_scan(
                                               o, a, b3, ii, op0=ALU.mult,
                                               op1=ALU.add)),
                                     waits=[e1, e2] + iw +
                                     (bufX.wr_waits() if s == 0 else []))
                        sc_toks.append(sct)
                        stt = P.emit("v", (lambda e, o=xst[s][:, m:m + 1],
                                           a=xs_[:, hi - 1:hi]:
                                           e.tensor_copy(o, a)),
                                     waits=[sct] +
                                     st_state[s][m].wr_waits())
                        st_state[s][m].did_write(stt)
                    bufA.did_read(sc_toks[-1])
                    bufB.did_read(sc_toks[-1])
                    bufX.w = sc_toks[-1]
                    bufX.r = [sc_toks[0]]
                    p1 = P.emit("g", (lambda e,
                                      o=xh[:, m * NW:(m + 1) * NW], a=xs_:
                                      e.tensor_copy(o, a)),
                                waits=sc_toks + xh_bufs[m].wr_waits())
                    bufX.did_read(p1)
                    xh_bufs[m].did_write(p1)
                    xs_info.append((m, xs_, bufX, sc_toks))
                for m, xs_, bufX, sc_toks in xs_info:
                    ucol = m * NW
                    slT, bufT = sc32.next()
                    t2_ = slT[:, 0:NW]
                    p2 = P.emit("v", (lambda e, o=t2_,
                                      a=u_bf[:, ucol:ucol + NW], b3=xs_:
                                      e.tensor_tensor(o, a, b3, op=ALU.mult)),
                                waits=sc_toks + u_ready[m] + bufT.wr_waits())
                    bufX.did_read(p2)
                    ubf_bufs[m].did_read(p2)
                    bufT.did_write(p2)
                    dtoks = []
                    for s in range(NSEQ):
                        lo, hi = s * T, (s + 1) * T
                        if it == 0:
                            init, iw = 1.0, []
                        else:
                            init = dst[s][:, m:m + 1]
                            iw = ds_state[s][m].rd_waits()
                        dct = P.emit("v", (lambda e,
                                           o=dsb[:, m * NW + lo:m * NW + hi],
                                           a=onesT[:, lo:hi],
                                           b3=t2_[:, lo:hi], ii=init:
                                           e.tensor_tensor_scan(
                                               o, a, b3, ii, op0=ALU.mult,
                                               op1=ALU.add)),
                                     waits=[p2, k2] + iw +
                                     (ds_bufs[m].wr_waits() if s == 0
                                      else []))
                        dtoks.append(dct)
                        stt = P.emit("v", (lambda e, o=dst[s][:, m:m + 1],
                                           a=dsb[:, m * NW + hi - 1:
                                                 m * NW + hi]:
                                           e.tensor_copy(o, a)),
                                     waits=[dct] +
                                     ds_state[s][m].wr_waits())
                        ds_state[s][m].did_write(stt)
                    bufT.did_read(dtoks[-1])
                    ds_bufs[m].w = dtoks[-1]
                    ds_bufs[m].r = [dtoks[0]]
            xs_ready = [xh_bufs[m].rd_waits() for m in range(DIC)]

            for mg in range(4):
                ms = list(range(mg * G, (mg + 1) * G))
                grp_c, gtok_c = mm_group(
                    (lambda k, mg=mg: s_wc[mg, k]),
                    16, ms,
                    rhs_fn=lambda k: xh[:, k * NW:(k + 1) * NW],
                    rhs_tok_fn=lambda k: xs_ready[k],
                    cast_tok=cast["wc"][mg])
                for kk in range(16):
                    xh_bufs[kk].did_read(gtok_c)
                grp_z = take_banks(G)
                sts = [load_st(s_wz[mg, k], waits=list(cast["wz"][mg]))
                       for k in range(8)]
                gtok_z = None
                for k in range(8):
                    ap, tok, slot = sts[k]
                    waits = [tok, xn_tok[0], xn_tok[1]]
                    for j in range(G):
                        b = grp_z[j]
                        for s in range(NSEQ):
                            rhs = xnb[s][:, k * W + 1:k * W + 1 + T]
                            last = (j == G - 1 and s == NSEQ - 1)
                            w = (list(waits) if (j == 0 and s == 0)
                                 else [])
                            if k == 0 and s == 0:
                                w += b["buf"].wr_waits()
                            kt = P.emit("t", (lambda e,
                                              o=b["ap"][:, s * T:(s + 1) * T],
                                              lh=ap[:, j * 128:(j + 1) * 128],
                                              r=rhs,
                                              st=(k == 0 and s == 0),
                                              sp=(k == 7 and s == NSEQ - 1):
                                              e.matmul(o, lh, r, start=st,
                                                       stop=sp)),
                                        waits=w, sig=last)
                            if last:
                                slot.did_read(kt)
                                gtok_z = kt
                for b in grp_z:
                    b["buf"].did_write(gtok_z)
                xn_bufs[0].did_read(gtok_z)
                xn_bufs[1].did_read(gtok_z)
                for j, m in enumerate(ms):
                    bc_, bz_ = grp_c[j], grp_z[j]
                    slT, bufT = sc32.next()
                    tz = slT[:, 0:NW]
                    y1 = P.emit("v", (lambda e, o=tz, a=bz_["ap"][:, 0:NW],
                                      b3=dsb[:, m * NW:(m + 1) * NW]:
                                      e.tensor_tensor(o, a, b3, op=ALU.mult)),
                                waits=[gtok_z, ds_bufs[m].w] + ds_bufs[m].r +
                                bufT.wr_waits())
                    bz_["buf"].did_read(y1)
                    ds_bufs[m].did_read(y1)
                    bufT.did_write(y1)
                    y2 = P.emit("v", (lambda e,
                                      o=dy[:, m * NW:(m + 1) * NW], a=tz,
                                      b3=bc_["ap"][:, 0:NW]:
                                      e.tensor_tensor(o, a, b3, op=ALU.mult)),
                                waits=[y1, gtok_c] + dy_bufs[m].wr_waits())
                    bc_["buf"].did_read(y2)
                    bufT.did_read(y2)
                    dy_bufs[m].did_write(y2)
            y_ready = [dy_bufs[m].rd_waits() for m in range(DIC)]

            acc2 = LNAcc(DMC, NW)
            for mg in range(2):
                grp, gtok = mm_group(
                    (lambda k, mg=mg: s_wo[mg, k]),
                    16, list(range(mg * G, (mg + 1) * G)),
                    rhs_fn=lambda k: dy[:, k * NW:(k + 1) * NW],
                    rhs_tok_fn=lambda k: y_ready[k],
                    cast_tok=cast["wo"][mg])
                for kk in range(16):
                    dy_bufs[kk].did_read(gtok)
                for j in range(G):
                    m = mg * G + j
                    b = grp[j]
                    t1 = P.emit("s", (lambda e,
                                      o=attn[:, m * NW:(m + 1) * NW],
                                      a=b["ap"][:, 0:NW]:
                                      e.activation(o, a, AF.Identity)),
                                waits=[gtok] + attn_bufs[m].wr_waits())
                    attn_bufs[m].did_write(t1)
                    slot, sbuf_ = sbf.next()
                    t2 = P.emit("s", (lambda e, o=slot[:, 512:1024],
                                      a=b["ap"][:, 0:NW]:
                                      e.activation(o, a, AF.Square)),
                                waits=[gtok] + sbuf_.wr_waits())
                    b["buf"].did_read(t1)
                    b["buf"].did_read(t2)
                    t3 = P.emit("g", (lambda e, o=slot[:, 0:512],
                                      a=attn[:, m * NW:(m + 1) * NW]:
                                      e.tensor_copy(o, a)),
                                waits=[t1] + sbuf_.wr_waits())
                    attn_bufs[m].did_read(t3)
                    sbuf_.w = t2
                    sbuf_.r = [t3]
                    acc2.add(slot, t3, t2, sbuf_)

            brs2, brm2, b1_2, b2_2 = acc2.finish()
            for m in range(DMC):
                slot, sbuf_ = sc32.next()
                a1 = P.emit("v", (lambda e, o=slot[:, 0:NW],
                                  a=attn[:, m * NW:(m + 1) * NW],
                                  b3=brs2["ap"][:, 0:NW]:
                                  e.tensor_tensor(o, a, b3, op=ALU.mult)),
                            waits=[attn_bufs[m].w, b1_2] + sbuf_.wr_waits())
                brs2["buf"].did_read(a1)
                attn_bufs[m].did_read(a1)
                sbuf_.did_write(a1)
                a2 = P.emit("v", (lambda e, o=slot[:, 0:NW], a=slot[:, 0:NW],
                                  b3=brm2["ap"][:, 0:NW]:
                                  e.tensor_tensor(o, a, b3, op=ALU.subtract)),
                            waits=[a1, b2_2])
                brm2["buf"].did_read(a2)
                htoks = []
                for s in range(NSEQ):
                    lo = s * T
                    h1 = P.emit("v", (lambda e,
                                      o=hb[:, m * NW + lo:m * NW + lo + T],
                                      a=slot[:, lo:lo + T],
                                      b3=xw[s][:, m * W + 1:m * W + 1 + T]:
                                      e.tensor_tensor(o, a, b3, op=ALU.add)),
                                waits=[a2] + xw_wtoks[s] +
                                (hb_bufs[m].wr_waits() if s == 0 else []))
                    xw_bufs[s].did_read(h1)
                    htoks.append(h1)
                sbuf_.did_read(htoks[0])
                sbuf_.did_read(htoks[1])
                hb_bufs[m].w = htoks[-1]
                hb_bufs[m].r = [htoks[0]]

            acc3 = LNAcc(DMC, NW)
            for m in range(DMC):
                slot, sbuf_ = sbf.next()
                hw = merge_toks([hb_bufs[m].w] + hb_bufs[m].r)
                t3 = P.emit("g", (lambda e, o=slot[:, 0:512],
                                  a=hb[:, m * NW:(m + 1) * NW]:
                                  e.tensor_copy(o, a)),
                            waits=hw + sbuf_.wr_waits())
                t2 = P.emit("s", (lambda e, o=slot[:, 512:1024],
                                  a=hb[:, m * NW:(m + 1) * NW]:
                                  e.activation(o, a, AF.Square)),
                            waits=hw + sbuf_.wr_waits())
                hb_bufs[m].did_read(t3)
                hb_bufs[m].did_read(t2)
                sbuf_.w = t2
                sbuf_.r = [t3]
                acc3.add(slot, t3, t2, sbuf_)
            brs3, brm3, b1_3, b2_3 = acc3.finish()
            hn_toks = []
            for m in range(DMC):
                slot, sbuf_ = sc32.next()
                a1 = P.emit("v", (lambda e, o=slot[:, 0:NW],
                                  a=hb[:, m * NW:(m + 1) * NW],
                                  b3=brs3["ap"][:, 0:NW]:
                                  e.tensor_tensor(o, a, b3, op=ALU.mult)),
                            waits=[hb_bufs[m].w, b1_3] + sbuf_.wr_waits())
                brs3["buf"].did_read(a1)
                hb_bufs[m].did_read(a1)
                sbuf_.did_write(a1)
                a2 = P.emit("v", (lambda e, o=xh[:, m * NW:(m + 1) * NW],
                                  a=slot[:, 0:NW], b3=brm3["ap"][:, 0:NW]:
                                  e.tensor_tensor(o, a, b3, op=ALU.subtract)),
                            waits=[a1, b2_3] + xh_bufs[m].wr_waits())
                brm3["buf"].did_read(a2)
                sbuf_.did_read(a2)
                xh_bufs[m].did_write(a2)
                hn_toks.append(a2)

            acc4 = LNAcc(DMC, NW)
            for mg in range(2):
                grp, gtok = mm_group(
                    (lambda k, mg=mg: s_wf[mg, k]),
                    8, list(range(mg * G, (mg + 1) * G)),
                    rhs_fn=lambda k: xh[:, k * NW:(k + 1) * NW],
                    rhs_tok_fn=lambda k: [hn_toks[k]],
                    cast_tok=cast["wf"][mg])
                for kk in range(DMC):
                    xh_bufs[kk].did_read(gtok)
                for j in range(G):
                    m = mg * G + j
                    b = grp[j]
                    t1 = P.emit("s", (lambda e,
                                      o=ffb[:, m * NW:(m + 1) * NW],
                                      a=b["ap"][:, 0:NW],
                                      bi=bselS[:, m:m + 1]:
                                      e.activation(o, a, AF.Identity,
                                                   bias=bi)),
                                waits=[gtok, g19] + ff_prolog_reads +
                                ff_bufs[m].wr_waits())
                    ff_bufs[m].did_write(t1)
                    slot, sbuf_ = sbf.next()
                    t2 = P.emit("s", (lambda e, o=slot[:, 512:1024],
                                      a=b["ap"][:, 0:NW],
                                      bi=bselS[:, m:m + 1]:
                                      e.activation(o, a, AF.Square, bias=bi)),
                                waits=[gtok, g19] + sbuf_.wr_waits())
                    b["buf"].did_read(t1)
                    b["buf"].did_read(t2)
                    t3 = P.emit("g", (lambda e, o=slot[:, 0:512],
                                      a=ffb[:, m * NW:(m + 1) * NW]:
                                      e.tensor_copy(o, a)),
                                waits=[t1] + sbuf_.wr_waits())
                    ff_bufs[m].did_read(t3)
                    sbuf_.w = t2
                    sbuf_.r = [t3]
                    acc4.add(slot, t3, t2, sbuf_)

            if it + 1 < NIT:
                head = emit_head(it + 1)

            brs4, brm4, b1_4, b2_4 = acc4.finish()
            out_toks = []
            for m in range(DMC):
                slot, sbuf_ = sc32.next()
                a1 = P.emit("v", (lambda e, o=slot[:, 0:NW],
                                  a=ffb[:, m * NW:(m + 1) * NW],
                                  b3=brs4["ap"][:, 0:NW]:
                                  e.tensor_tensor(o, a, b3, op=ALU.mult)),
                            waits=[ff_bufs[m].w, b1_4] + sbuf_.wr_waits())
                brs4["buf"].did_read(a1)
                ff_bufs[m].did_read(a1)
                sbuf_.did_write(a1)
                a2 = P.emit("v", (lambda e, o=slot[:, 0:NW], a=slot[:, 0:NW],
                                  b3=brm4["ap"][:, 0:NW]:
                                  e.tensor_tensor(o, a, b3, op=ALU.subtract)),
                            waits=[a1, b2_4])
                brm4["buf"].did_read(a2)
                a3 = P.emit("v", (lambda e, o=attn[:, m * NW:(m + 1) * NW],
                                  a=slot[:, 0:NW],
                                  b3=hb[:, m * NW:(m + 1) * NW]:
                                  e.tensor_tensor(o, a, b3, op=ALU.add)),
                            waits=[a2, hb_bufs[m].w] +
                            attn_bufs[m].wr_waits())
                hb_bufs[m].did_read(a3)
                sbuf_.did_read(a3)
                attn_bufs[m].did_write(a3)
                out_toks.append(a3)

            av = attn[:, :].rearrange("p (c w) -> p c w", w=NW)
            for s in range(NSEQ):
                tok = dma("s", oview[s][:, :, tk:tk + T],
                          av[:, :, s * T:(s + 1) * T], waits=out_toks,
                          sem="do")
                out_dma_toks.append(tok)
                for m in range(DMC):
                    attn_bufs[m].did_read(tok)

            if it + 1 < NIT:
                xn_tok = emit_ln1_apply(head)

        P.wait_only("y", out_dma_toks)

        def replay(e, name):
            for ws, fn, tok in P.stream[name]:
                for semn, val in ws:
                    e.wait_ge(sems[semn], val)
                if fn is None:
                    continue
                inst = fn(e)
                if tok is not None:
                    semn = tok[0]
                    inst.then_inc(sems[semn], 16 if semn in DMA_SEMS else 1)

        @block.tensor
        def _(e):
            replay(e, "t")

        @block.vector
        def _(e):
            replay(e, "v")

        @block.scalar
        def _(e):
            replay(e, "s")

        @block.gpsimd
        def _(e):
            replay(e, "g")

        @block.sync
        def _(e):
            replay(e, "y")

    return nc


_NC_CACHE = {}


def _get_nc():
    if "nc" not in _NC_CACHE:
        _NC_CACHE["nc"] = build_program()
    return _NC_CACHE["nc"]


def kernel(L0_attn_x, L0_attn_W_in, L0_attn_conv_w, L0_attn_conv_b,
           L0_attn_W_dt, L0_attn_A_proj, L0_attn_B_proj, L0_attn_C_proj,
           L0_attn_W_out, L0_ffn_W, L0_ffn_b, L0_ffn_logits, L0_ffn_u):
    x = np.asarray(L0_attn_x, dtype=np.float32)
    xt = np.ascontiguousarray(x.transpose(0, 2, 1))

    w_in = np.asarray(L0_attn_W_in, np.float32)
    wu_h = np.ascontiguousarray(w_in[:DI, :].T)
    wz_h = np.ascontiguousarray(w_in[DI:, :].T)
    wdt_h = np.ascontiguousarray(np.asarray(L0_attn_W_dt, np.float32).T)
    wap_h = np.ascontiguousarray(np.asarray(L0_attn_A_proj, np.float32).T)
    wbp_h = np.ascontiguousarray(np.asarray(L0_attn_B_proj, np.float32).T)
    wcp_h = np.ascontiguousarray(np.asarray(L0_attn_C_proj, np.float32))
    wo_h = np.ascontiguousarray(np.asarray(L0_attn_W_out, np.float32).T)
    wff_h = np.ascontiguousarray(np.asarray(L0_ffn_W, np.float32))
    bff_h = np.ascontiguousarray(np.asarray(L0_ffn_b, np.float32))
    cw_h = np.ascontiguousarray(
        np.asarray(L0_attn_conv_w, np.float32).reshape(16, 128, 3)
        .transpose(1, 0, 2).reshape(128, 48))
    cb_h = np.ascontiguousarray(
        np.asarray(L0_attn_conv_b, np.float32).reshape(16, 128).T)
    lg_h = np.ascontiguousarray(np.asarray(L0_ffn_logits, np.float32))
    uu_h = np.ascontiguousarray(np.asarray(L0_ffn_u, np.float32))

    nc = _get_nc()
    in_maps = []
    for c in range(NCORES):
        in_maps.append({
            "x": xt[c * NSEQ:(c + 1) * NSEQ],
            "wu": wu_h, "wz": wz_h, "wdt": wdt_h, "wap": wap_h,
            "wbp": wbp_h, "wcp": wcp_h, "wo": wo_h, "wff": wff_h,
            "bff": bff_h, "cw": cw_h, "cb": cb_h, "lg8": lg_h, "uu8": uu_h,
        })
    res = run_bass_kernel_spmd(nc, in_maps, core_ids=list(range(NCORES)))
    parts = [r["o"] for r in res.results]
    out = np.concatenate(parts, axis=0)
    return np.ascontiguousarray(out.transpose(0, 2, 1)).astype(np.float32)
